# version 02 (final)
# baseline (speedup 1.0000x reference)
import math
import numpy as np
import ml_dtypes

import concourse.bass as bass
import concourse.mybir as mybir
from concourse.bass_utils import run_bass_kernel_spmd

F32 = mybir.dt.float32
BF16 = mybir.dt.bfloat16
AF = mybir.ActivationFunctionType
ALU = mybir.AluOpType
AX = mybir.AxisListType.X
BF = ml_dtypes.bfloat16

D = 1024
NH = 16
HD = 64
WIN = 256
DI = 2730
NKC2 = 22
QK_SCALE = 8.0
MLA_DR = 32
MLA_DC = 128
MLA_SCALE = 1.0 / math.sqrt(HD + MLA_DR)


def make_cfg(full=True):
    if full:
        B, T, depth = 2, 8192, 6
    else:
        B, T, depth = 1, 2048, 2
    cfg = dict(B=B, T=T, depth=depth)
    cfg["G"] = 4
    cfg["n_cores"] = 4 * B
    cfg["TC"] = T // 4
    cfg["NT"] = cfg["TC"] // 128
    cfg["NKV"] = T // 128
    cfg["SB_KV"] = 8
    return cfg



def host_constants(cfg):
    f32, f64 = np.float32, np.float64
    T = cfg["T"]
    c = {}
    ident = np.eye(128, dtype=f32)
    c["c_ident"] = ident
    sh = np.zeros((128, 128), f32); sh[np.arange(127), np.arange(1, 128)] = 1
    c["c_shift"] = sh
    ee = np.zeros((128, 128), f32); ee[127, 0] = 1
    c["c_elast"] = ee
    c["c_negident"] = -ident
    c["c_ones2"] = np.ones((2, 128), f32)
    c["c_ones1"] = np.ones((1, 128), f32)
    c["c_onescol"] = np.ones((128, 1), f32)
    ii = np.arange(128)[:, None]; jj = np.arange(384)[None, :]
    c["c_band"] = ((jj >= ii) & (jj <= ii + 256)).astype(f32)
    kv = np.arange(128)[:, None]; q = np.arange(128)[None, :]
    c["c_mla_diag"] = (kv <= q).astype(f32)
    sel = np.zeros((8, 128, 16), f32)
    for ch in range(8):
        sel[ch, 0:64, 2 * ch] = 1
        sel[ch, 64:128, 2 * ch + 1] = 1
    c["c_sel"] = sel.reshape(8 * 128, 16)
    c["c_selrope"] = np.ones((32, 16), f32)
    repsel = np.zeros((16, 8 * 128), f32)
    for hp in range(8):
        for rix in range(128):
            repsel[2 * hp + rix // 64, hp * 128 + rix] = 1
    c["c_repsel"] = repsel
    half = np.arange(0, HD, 2, dtype=f32)
    inv_freq = (1.0 / (10000.0 ** (half.astype(f64) / HD))).astype(f32)
    t = np.arange(T, dtype=f32)
    freqs = (t[:, None] * inv_freq[None, :]).astype(f32)
    freqs = np.concatenate([freqs, freqs], -1)
    base_scale = ((half + 0.4 * HD) / (1.4 * HD)).astype(f32)
    power = ((t - T // 2) / (WIN // 2)).astype(f32)
    scale = (base_scale[None, :].astype(f64) ** power[:, None].astype(f64)).astype(f32)
    scale = np.concatenate([scale, scale], -1)
    cos = np.cos(freqs.astype(f64)).astype(f32)
    sin = np.sin(freqs.astype(f64)).astype(f32)
    c["t_qcos"] = (cos.astype(f64) * scale.astype(f64) * QK_SCALE).astype(f32)
    c["t_qsin"] = (sin.astype(f64) * scale.astype(f64) * QK_SCALE).astype(f32)
    c["t_kcos"] = (cos.astype(f64) / scale.astype(f64)).astype(f32)
    c["t_ksin"] = (sin.astype(f64) / scale.astype(f64)).astype(f32)
    inv_freq_m = (1.0 / (10000.0 ** (np.arange(0, MLA_DR, 2, dtype=f64) / MLA_DR))).astype(f32)
    freqs_m = (t[:, None] * inv_freq_m[None, :]).astype(f32)
    c["t_mcos"] = np.cos(freqs_m.astype(f64)).astype(f32)
    c["t_msin"] = np.sin(freqs_m.astype(f64)).astype(f32)
    c["t_mcos_q"] = (c["t_mcos"].astype(f64) * MLA_SCALE).astype(f32)
    c["t_msin_q"] = (c["t_msin"].astype(f64) * MLA_SCALE).astype(f32)
    return c


def percore_inputs(cfg, inputs, consts):
    B, T, TC, NT, G = cfg["B"], cfg["T"], cfg["TC"], cfg["NT"], cfg["G"]
    x = np.ascontiguousarray(inputs["L0_attn_x"]).astype(np.float32)
    maps = []
    for core in range(cfg["n_cores"]):
        b, rc = core // G, core % G
        s = rc * TC
        m = {}
        m["x"] = np.ascontiguousarray(x[b, s:s + TC])
        lo = max(s - 256, 0)
        def tab(name, width=64):
            tt = consts[name][lo:s + TC]
            if tt.shape[0] < TC + 256:
                tt = np.concatenate(
                    [np.zeros((TC + 256 - tt.shape[0], width), np.float32), tt], 0)
            return np.ascontiguousarray(tt)
        m["t_qcos"] = tab("t_qcos"); m["t_qsin"] = tab("t_qsin")
        m["t_kcos"] = tab("t_kcos"); m["t_ksin"] = tab("t_ksin")
        qt = np.zeros((NT, 128, 16), np.float32); qt2 = np.zeros((NT, 128, 16), np.float32)
        for mm in range(NT):
            j = 4 * mm + rc
            qt[mm] = consts["t_mcos_q"][j * 128:(j + 1) * 128]
            qt2[mm] = consts["t_msin_q"][j * 128:(j + 1) * 128]
        m["t_mcos_q"] = qt.reshape(NT * 128, 16)
        m["t_msin_q"] = qt2.reshape(NT * 128, 16)
        m["t_mcos_k"] = np.ascontiguousarray(consts["t_mcos"][:T])
        m["t_msin_k"] = np.ascontiguousarray(consts["t_msin"][:T])
        bm = np.broadcast_to(consts["c_band"], (2, 128, 384)).copy()
        if rc == 0:
            bm[0][:, :256] = 0
            bm[1][:, :128] = 0
        m["bmask"] = bm.reshape(2 * 128, 384).astype(np.float32)
        m["hblend"] = np.full((1, 1), 0.0 if rc == 0 else 1.0, np.float32)
        vm = np.zeros((NT, 4, 128, 128), np.float32)
        for mm in range(NT):
            qt_idx = 4 * mm + rc
            for j in range(4):
                kvt = 4 * mm + j
                if kvt < qt_idx:
                    vm[mm, j] = 1.0
                elif kvt == qt_idx:
                    vm[mm, j] = consts["c_mla_diag"]
        m["mla_vmask"] = vm.reshape(NT * 4 * 128, 128)
        for k, v in consts.items():
            if k.startswith("c_"):
                m[k] = v
        for k in inputs:
            if k != "L0_attn_x":
                m[k] = inputs[k]
        maps.append(m)
    return maps



ENGS = ("pe", "act", "dve", "pool", "sp")

class Ctx:
    def __init__(self, nc):
        self.nc = nc
        self.q = {e: [] for e in ENGS}
        self.psem = {e: nc.alloc_semaphore(f"prog_{e}") for e in ENGS}
        self.pcnt = {e: 0 for e in ENGS}
        self.dsems = {}
        self.last_w = {}
        self.readers = {}
        self.floor = {e: {} for e in ENGS}
        self.csem_n = 0

    def dstream(self, eng, stream, R=4):
        k = (eng, stream)
        if k not in self.dsems:
            self.dsems[k] = {
                "sems": [self.nc.alloc_semaphore(f"dma_{eng}_{stream}{i}") for i in range(R)],
                "n": 0, "R": R}
        return self.dsems[k]

    def _add_wait(self, e, waits, tok):
        sem, val = tok
        if self.floor[e].get(sem.num, 0) >= val:
            return
        cur = waits.get(sem.num)
        waits[sem.num] = (sem, max(val, cur[1]) if cur else val)

    def _deps(self, e, reads, writes):
        waits = {}
        for k in reads:
            w = self.last_w.get(k)
            if w is not None:
                self._add_wait(e, waits, w)
        for k in writes:
            w = self.last_w.get(k)
            if w is not None:
                self._add_wait(e, waits, w)
            for tok in self.readers.get(k, {}).values():
                self._add_wait(e, waits, tok)
        if e == "pe":
            waits.pop(self.psem["pe"].num, None)
        return waits

    def _commit(self, e, tok, reads, writes, waits):
        for k in writes:
            self.last_w[k] = tok
            self.readers.pop(k, None)
        for k in reads:
            self.readers.setdefault(k, {})[tok[0].num] = tok
        for num, (sem, val) in waits.items():
            self.floor[e][num] = max(self.floor[e].get(num, 0), val)

    def op(self, e, fn, reads=(), writes=(), tag=""):
        waits = self._deps(e, reads, writes)
        self.pcnt[e] += 1
        tok = (self.psem[e], self.pcnt[e])
        self._commit(e, tok, reads, writes, waits)
        self.q[e].append((list(waits.values()), fn, tok, 1, tag))
        return tok

    def dma(self, e, fn, reads=(), writes=(), stream="d", tag=""):
        waits = self._deps(e, reads, writes)
        ds = self.dstream(e, stream)
        i, R = ds["n"], ds["R"]
        ds["n"] += 1
        sem = ds["sems"][i % R]
        tok = (sem, 16 * (i // R + 1))
        if i >= R:
            self._add_wait(e, waits, (sem, 16 * (i // R)))
        self._commit(e, tok, reads, writes, waits)
        self.q[e].append((list(waits.values()), fn, tok, 16, tag))
        return tok

    def coll(self, fn, reads=(), writes=(), tag="coll"):
        e = "pool"
        waits = self._deps(e, reads, writes)
        sem = self.nc.alloc_semaphore(f"coll{self.csem_n}")
        self.csem_n += 1
        tok = (sem, 1)
        self._commit(e, tok, reads, writes, waits)
        self.q[e].append((list(waits.values()), fn, tok, 1, tag))
        return tok

    def fence(self, engines=ENGS, tag="fence"):
        snap = [(self.psem[e], self.pcnt[e]) for e in ENGS if self.pcnt[e] > 0]
        for ds in self.dsems.values():
            n, R = ds["n"], ds["R"]
            for j in range(min(n, R)):
                i = n - 1 - j
                snap.append((ds["sems"][i % R], 16 * (i // R + 1)))
        for e in engines:
            waits = {}
            for tok in snap:
                self._add_wait(e, waits, tok)
            if waits:
                self.q[e].append((list(waits.values()), None, None, 0, tag))
                for num, (sem, val) in waits.items():
                    self.floor[e][num] = max(self.floor[e].get(num, 0), val)

    def wait_tok(self, e, tok, tag="w"):
        waits = {}
        self._add_wait(e, waits, tok)
        if waits:
            self.q[e].append((list(waits.values()), None, None, 0, tag))
            for num, (sem, val) in waits.items():
                self.floor[e][num] = max(self.floor[e].get(num, 0), val)

    def finalize(self, block):
        def emit_for(e):
            def run(engine):
                for waits, fn, tok, inc, tag in self.q[e]:
                    for (sem, val) in waits:
                        engine.wait_ge(sem, val)
                    if fn is None:
                        continue
                    inst = fn(engine)
                    if tok is not None:
                        inst.then_inc(tok[0], inc)
            return run
        block.tensor(emit_for("pe"))
        block.scalar(emit_for("act"))
        block.vector(emit_for("dve"))
        block.gpsimd(emit_for("pool"))
        block.sync(emit_for("sp"))



class Build:
    def __init__(self, cfg, upto="full", sub=None):
        self.cfg = cfg
        self.upto = upto
        self.sub = sub
        nc = bass.Bass("TRN2", target_bir_lowering=False)
        self.nc = nc
        self.c = Ctx(nc)
        self.S = {}
        self.Dr = {}
        self._sv = {}
        self._declare_io()
        self._alloc_static()
        self._emit()
        self.c.fence(engines=("pool",), tag="final")
        self.block = nc.Block()
        self.block.__enter__()
        self.c.finalize(self.block)
        self.block.__exit__(None, None, None)

    def _declare_io(self):
        nc, cfg = self.nc, self.cfg
        TC, NT, T, dep, G = cfg["TC"], cfg["NT"], cfg["T"], cfg["depth"], cfg["G"]
        dt = nc.dram_tensor
        self.Dr["x"] = dt("x", [TC, D], F32, kind="ExternalInput")
        self.Dr["out"] = dt("out", [TC, D], F32, kind="ExternalOutput")
        for n, shp in [
            ("L0_attn_W_kv_down", [D, MLA_DC]), ("L0_attn_W_k_up", [MLA_DC, D]),
            ("L0_attn_W_v_up", [MLA_DC, D]), ("L0_attn_W_q", [D, D]),
            ("L0_attn_W_q_rope", [D, MLA_DR]), ("L0_attn_W_k_rope", [MLA_DC, MLA_DR]),
            ("L0_attn_W_o", [D, D]),
            ("L0_ffn_Wqkv", [dep, D, 3 * D]), ("L0_ffn_q_scale", [dep, HD]),
            ("L0_ffn_k_scale", [dep, HD]), ("L0_ffn_Wo", [dep, D, D]),
            ("L0_ffn_ln_g", [dep, D]), ("L0_ffn_ln_b", [dep, D]),
            ("L0_ffn_W1", [dep, D, 2 * DI]), ("L0_ffn_b1", [dep, 2 * DI]),
            ("L0_ffn_W2", [dep, DI, D]), ("L0_ffn_b2", [dep, D]),
            ("c_ident", [128, 128]), ("c_shift", [128, 128]), ("c_elast", [128, 128]),
            ("c_negident", [128, 128]), ("c_ones2", [2, 128]), ("c_ones1", [1, 128]),
            ("c_onescol", [128, 1]), ("c_band", [128, 384]), ("c_mla_diag", [128, 128]),
            ("c_sel", [8 * 128, 16]), ("c_selrope", [32, 16]), ("c_repsel", [16, 8 * 128]),
            ("t_qcos", [TC + 256, 64]), ("t_qsin", [TC + 256, 64]),
            ("t_kcos", [TC + 256, 64]), ("t_ksin", [TC + 256, 64]),
            ("t_mcos_q", [NT * 128, 16]), ("t_msin_q", [NT * 128, 16]),
            ("t_mcos_k", [T, 16]), ("t_msin_k", [T, 16]),
            ("bmask", [2 * 128, 384]), ("hblend", [1, 1]),
            ("mla_vmask", [NT * 4 * 128, 128]),
        ]:
            self.Dr[n] = dt(n, shp, F32, kind="ExternalInput")
        idt = lambda n, shp, dty=F32: dt(n, shp, dty, kind="Internal")
        self.Dr["park1"] = idt("park1", [TC, D])
        self.Dr["park2"] = idt("park2", [TC, D])
        self.Dr["mla_o"] = idt("mla_o", [NH * (HD + 1), TC])
        self.Dr["h1q"] = idt("h1q", [D, TC], BF16)
        W = self.cfg["n_cores"]
        self.Dr["ag_u_in"] = idt("ag_u_in", [1028, 256], BF16)
        self.Dr["ag_u_out"] = idt("ag_u_out", [W * 1028, 256], BF16)
        self.Dr["ag_hl_in"] = idt("ag_hl_in", [8, 128], BF16)
        self.Dr["ag_hl_out"] = idt("ag_hl_out", [W * 8, 128], BF16)
        self.Dr["ag_h1_in"] = idt("ag_h1_in", [D, TC], BF16)
        self.Dr["ag_h1_out"] = idt("ag_h1_out", [W * D, TC], BF16)
        self.Dr["ag_mla_in"] = idt("ag_mla_in", [TC, D])
        self.Dr["ag_mla_out"] = idt("ag_mla_out", [W * TC, D])
        self.Dr["h1loc"] = idt("h1loc", [G * D, TC], BF16)
        self.Dr["mlaloc"] = idt("mlaloc", [G * TC, D])

    def _alloc_static(self):
        nc, cfg = self.nc, self.cfg
        NT = cfg["NT"]
        A = lambda n, shp, dty: nc.alloc_sbuf_tensor("s_" + n, shp, dty)
        S = self.S
        S["h"] = A("h", [128, NT, D], F32)
        for n, shp, dty in [
            ("ident", [128, 128], BF16), ("shift", [128, 128], BF16),
            ("elast", [128, 128], BF16), ("negident", [128, 128], BF16),
            ("ones2", [2, 128], BF16), ("ones1", [1, 128], BF16),
            ("onescol", [128, 1], BF16), ("band", [128, 384], BF16),
            ("bmask", [128, 2, 384], BF16), ("sel", [128, 8, 16], BF16),
            ("selrope", [32, 16], BF16), ("repsel", [16, 8 * 128], BF16),
            ("qcos", [128, NT + 2, 64], BF16), ("qsin", [128, NT + 2, 64], BF16),
            ("kcos", [128, NT + 2, 64], BF16), ("ksin", [128, NT + 2, 64], BF16),
            ("hblend", [1, 1], F32),
            ("qsrep", [128, HD], F32), ("ksrep", [128, HD], F32),
            ("st", [128, 16, NT], F32),
            ("sml", [128, 64], F32),
            ("hlast", [1, D], BF16), ("hprev", [128, D], BF16),
            ("inv", [128, 32], F32), ("rden", [128, 16], F32),
            ("sml1b", [128, 256], BF16), ("hblr", [128, 1], F32),
            ("hblendb", [1, 1], BF16),
            ("scr1k", [128, D], F32), ("konst", [128, 8], F32),
            ("bscr", [128, D], BF16), ("bscr2", [128, D], BF16),
        ]:
            S[n] = A(n, shp, dty)
        rem = nc.sbuf_bytes_remaining
        self.AB = rem - 2048
        r = nc.bump_sbuf(self.AB, None, 64)
        assert r is not None, "arena alloc failed"
        self.abase = r[0]
        self._carves = {}
        self._phase_defs()

    def _phase_defs(self):
        cfg = self.cfg
        NT = cfg["NT"]
        kb = 1024
        def carve(pfx, items, base=0, name_pfx=None):
            off = base
            for n, shp, dty in items:
                per_part = int(np.prod(shp[1:])) * (2 if dty == BF16 else 4)
                nm = (name_pfx or pfx) + "_" + n
                t = self.nc.alloc_sbuf_tensor_at(nm, shp, dty, offset=self.abase + off)
                self.S[nm] = t
                off += (per_part + 63) // 64 * 64
            assert off <= self.AB, f"phase {pfx} over arena: {off} > {self.AB}"
            return off
        a = carve("at", [
            ("Wqkv", [128, 8, 3 * D], BF16),
            ("Wo", [128, 8, D], BF16),
            ("fq", [128, NT + 2, 64], BF16), ("fq2", [128, NT + 2, 64], BF16),
            ("fk", [128, NT + 2, 64], BF16), ("fk2", [128, NT + 2, 64], BF16),
            ("hb", [128, 2, D], BF16),
            ("ubT", [128, 1, D], BF16), ("uh", [128, 2, D], BF16),
            ("qb", [128, D], BF16), ("kb", [128, D], BF16),
            ("qT", [128, 1, D], BF16), ("kT", [128, 3, D], BF16), ("vT", [128, 3, D], BF16),
            ("rtmp", [128, D], BF16), ("rtmp2", [128, D], BF16),
            ("P", [128, 2, 384], BF16), ("PT", [128, 2, 3, 128], BF16),
            ("attT", [128, 8, 128], BF16),
        ])
        f = carve("ff", [
            ("W1", [128, 8, 1024], BF16),
            ("W2", [128, NKC2, D], BF16),
            ("xnT", [128, 8, 512], BF16),
            ("h2T", [128, NKC2, 512], BF16),
            ("xa", [128, 2, D], BF16),
            ("grep", [128, D], BF16),
            ("b1hl", [2, 2 * DI], BF16), ("b2hl", [2, D], BF16), ("blnhl", [2, D], BF16),
                    ])
        NKV, TC, SBK = cfg["NKV"], cfg["TC"], cfg["SB_KV"]
        mp = carve("ml", [
            ("ckvT", [128, NKV * 128], BF16),
            ("qT", [128, 8, TC], BF16),
            ("qrb", [128, TC], BF16),
            ("mcq", [128, NT, 16], BF16), ("msq", [128, NT, 16], BF16),
            ("mck", [128, NKV, 16], BF16), ("msk", [128, NKV, 16], BF16),
            ("wkvd", [128, 8, 128], BF16), ("wkup", [128, 1, 1024], BF16),
            ("wvup", [128, 1, 1024], BF16), ("wkro", [128, 1, 32], BF16),
            ("rsc", [128, 64], F32), ("sq2", [128, 512], BF16),
            ("nrm", [128, 64], F32), ("nrmb", [128, 256], BF16),
            ("k2r", [128, 1], F32), ("rro", [128, 256], BF16),
            ("PT", [128, 2, 256], BF16),
            ("vm", [128, 8, 128], BF16),
        ])
        mc = carve("mlC", [
            ("wq", [128, 8, 1024], BF16), ("wqro", [128, 8, 32], BF16),
            ("hrhs", [128, 8, 512], BF16),
        ], base=mp, name_pfx="ml")
        md = carve("mlD", [
            ("kT", [128, 8, SBK * 128], BF16),
            ("krb", [128, SBK * 128], BF16),
            ("vT", [128, SBK, 16 * 65], BF16),
            ("ost", [128, 4, 256], F32),
            ("mlo", [128, 4, 256], F32),
        ], base=mp, name_pfx="ml")
        me = carve("mlE", [
            ("wo", [128, 8, 1024], BF16),
            ("obuf", [128, 8, 256], F32), ("obf", [128, 8, 128], BF16),
            ("den16", [16, 512], F32), ("denb", [16, 256], BF16),
        ], base=mp, name_pfx="ml")
        self.phase_sizes = dict(at=a, ff=f, mlP=mp, mlC=mc, mlD=md, mlE=me)

    def kk(self, val):
        if not hasattr(self, "_kvals"):
            self._kvals = {}
        if val not in self._kvals:
            i = len(self._kvals)
            assert i < 8
            self._kvals[val] = i
            self.c.op("pool", (lambda i_, v_: lambda g: g.memset(
                self.S["konst"][:, i_:i_ + 1], v_))(i, float(val)),
                writes=("konst",), tag=f"konst{i}")
        return self.S["konst"][:, self._kvals[val]:self._kvals[val] + 1]

    def psum(self, name, shp, dty=F32):
        if not hasattr(self, "_psums"):
            self._psums = {}
        key = (name,)
        if key not in self._psums:
            self._psums[key] = self.nc.alloc_psum_tensor(name, shp, dty)
        return self._psums[key]

    def _emit(self):
        self._load_consts()
        self._prologue_ln()
        if self.upto == "ln":
            self._store_h_as_out()
            return
        dep = self.cfg["depth"]
        if self.upto == "attn0":
            self._inner_attn("B", 0)
            self._store_h_as_out()
            return
        if self.upto == "layer0":
            self._inner_layer("B", 0)
            self._store_h_as_out()
            return
        for li in range(dep):
            self._inner_layer("B", li)
        if self.upto == "stageB":
            self._store_h_as_out()
            return
        self._stage_boundary_B_end()
        self._mla()
        if self.upto == "mla":
            self._store_h_as_out()
            return
        self._stage_boundary_D()
        for li in range(dep):
            self._inner_layer("D", li)
        self._epilogue()

    def _load_consts(self):
        c, S, Dr = self.c, self.S, self.Dr
        NT = self.cfg["NT"]
        for n_s, n_d in [("ident", "c_ident"), ("shift", "c_shift"),
                         ("elast", "c_elast"), ("negident", "c_negident"),
                         ("ones2", "c_ones2"), ("ones1", "c_ones1"),
                         ("onescol", "c_onescol"),
                         ("selrope", "c_selrope"), ("repsel", "c_repsel")]:
            c.dma("pool", (lambda s_, d_: lambda g: g.dma_start(out=self.S[s_][:], in_=self.Dr[d_][:]))(n_s, n_d),
                  writes=(n_s,), stream="w", tag=f"ld_{n_s}")
        c.dma("pool", lambda g: g.dma_start(
            out=S["bmask"][:, :, :],
            in_=Dr["bmask"][:].rearrange("(a p) n -> p a n", p=128)),
            writes=("bmask",), stream="w", tag="ld_bmask")
        c.dma("pool", lambda g: g.dma_start(
            out=S["sel"][:, :, :],
            in_=Dr["c_sel"][:].rearrange("(a p) n -> p a n", p=128)),
            writes=("sel",), stream="w", tag="ld_sel")
        c.dma("pool", lambda g: g.dma_start(out=S["band"][:, :], in_=Dr["c_band"][:]),
              writes=("band",), stream="w", tag="ld_band")
        for n_s, n_d in [("qcos", "t_qcos"), ("qsin", "t_qsin"),
                         ("kcos", "t_kcos"), ("ksin", "t_ksin")]:
            c.dma("pool", (lambda s_, d_: lambda g: g.dma_start(
                out=self.S[s_][:, :, :],
                in_=self.Dr[d_][:].rearrange("(a p) n -> p a n", p=128)))(n_s, n_d),
                writes=(n_s,), stream="w", tag=f"ld_{n_s}")
        c.dma("sp", lambda g: g.dma_start(out=S["hblend"][:], in_=Dr["hblend"][:]),
              writes=("hblend",), stream="t", tag="ld_hblend")
        c.dma("pool", lambda g: g.dma_start(out=S["hblendb"][:], in_=Dr["hblend"][:]),
              writes=("hblendb",), stream="w", tag="ld_hblendb")
        c.op("pool", lambda g: g.memset(S["hprev"][:, :], 0.0),
             writes=("hprev",), tag="z_hprev")
        PSh = self.psum("PS", [128, 8, 512], F32)
        c.op("pe", lambda t: t.matmul(PSh[:, 7, 0:1], S["ones1"][:, :], S["hblendb"][:, :],
                                      start=True, stop=True),
             reads=("ones1", "hblendb"), writes=("pb7",), tag="hblrmm")
        c.op("dve", lambda v: v.tensor_copy(out=S["hblr"][:, :], in_=PSh[:, 7, 0:1]),
             reads=("pb7",), writes=("hblr",), tag="hblrev")

    def _ln_stats_plain(self, src_keyfn, src_apfn, stat_col):
        c, S = self.c, self.S
        NT = self.cfg["NT"]
        st = S["st"]
        for t in range(NT):
            ap = src_apfn(t)
            key = src_keyfn(t)
            c.op("dve", (lambda ap_, t_: lambda v: v.reduce_sum(
                out=st[:, 8, t_:t_ + 1], in_=ap_, axis=AX))(ap, t),
                reads=(key,), writes=(f"st8_{t}",), tag=f"lnsum{t}")
            c.op("act", (lambda ap_, t_: lambda s: s.activation(
                out=S["scr1k"][:, :], in_=ap_,
                func=AF.Square, accum_out=st[:, 9, t_:t_ + 1]))(ap, t),
                reads=(key,), writes=(f"st9_{t}", "scr1k"), tag=f"lnsq{t}")
            c.op("dve", (lambda t_: lambda v: v.tensor_scalar(
                out=st[:, 10, t_:t_ + 1], in0=st[:, 8, t_:t_ + 1],
                scalar1=1.0 / D, scalar2=None, op0=ALU.mult))(t),
                reads=(f"st8_{t}",), writes=(f"st10_{t}",), tag=f"lnmu{t}")
            c.op("dve", (lambda t_: lambda v: v.scalar_tensor_tensor(
                out=st[:, 11, t_:t_ + 1], in0=st[:, 10, t_:t_ + 1], scalar=-1.0,
                in1=st[:, 10, t_:t_ + 1], op0=ALU.mult, op1=ALU.mult))(t),
                reads=(f"st10_{t}",), writes=(f"st11_{t}",), tag=f"lnmu2{t}")
            c.op("dve", (lambda t_: lambda v: v.scalar_tensor_tensor(
                out=st[:, 11, t_:t_ + 1], in0=st[:, 9, t_:t_ + 1], scalar=1.0 / D,
                in1=st[:, 11, t_:t_ + 1], op0=ALU.mult, op1=ALU.add))(t),
                reads=(f"st9_{t}", f"st11_{t}"), writes=(f"st11_{t}",), tag=f"lnvar{t}")
            c.op("dve", (lambda t_: lambda v: v.tensor_scalar_max(
                out=st[:, 11, t_:t_ + 1], in0=st[:, 11, t_:t_ + 1], scalar1=0.0))(t),
                reads=(f"st11_{t}",), writes=(f"st11_{t}",), tag=f"lnclamp{t}")
            keps = self.kk(1e-5); kmh = self.kk(-0.5)
            c.op("act", (lambda t_, k_: lambda s: s.activation(
                out=st[:, 11, t_:t_ + 1], in_=st[:, 11, t_:t_ + 1], func=AF.Ln,
                bias=k_))(t, keps),
                reads=(f"st11_{t}", "konst"), writes=(f"st11_{t}",), tag=f"lnln{t}")
            c.op("act", (lambda t_, col, k_: lambda s: s.activation(
                out=st[:, col, t_:t_ + 1], in_=st[:, 11, t_:t_ + 1], func=AF.Exp,
                scale=k_))(t, stat_col + 1, kmh),
                reads=(f"st11_{t}", "konst"), writes=(f"st{stat_col + 1}_{t}",), tag=f"lnrstd{t}")
            c.op("dve", (lambda t_, col: lambda v: v.tensor_copy(
                out=st[:, col, t_:t_ + 1], in_=st[:, 10, t_:t_ + 1]))(t, stat_col),
                reads=(f"st10_{t}",), writes=(f"st{stat_col}_{t}",), tag=f"lnmuc{t}")

    def _prologue_ln(self):
        c, S, Dr = self.c, self.S, self.Dr
        NT = self.cfg["NT"]
        xt = Dr["x"].ap().rearrange("(a p) n -> p a n", p=128)
        for t in range(NT):
            c.dma("sp", (lambda t_: lambda g: g.dma_start(
                out=S["h"][:, t_], in_=xt[:, t_]))(t),
                writes=(f"h{t}",), stream="x", tag=f"ldx{t}")
        self._ln_stats_plain(lambda t: f"h{t}", lambda t: S["h"][:, t], 0)
        for t in range(NT):
            c.op("dve", (lambda t_: lambda v: v.scalar_tensor_tensor(
                out=S["st"][:, 2, t_:t_ + 1], in0=S["st"][:, 0, t_:t_ + 1],
                scalar=-1.0, in1=S["st"][:, 1, t_:t_ + 1],
                op0=ALU.mult, op1=ALU.mult))(t),
                reads=(f"st0_{t}", f"st1_{t}"), writes=(f"st2_{t}",), tag=f"pnm{t}")
            c.op("dve", (lambda t_: lambda v: v.tensor_scalar(
                out=S["h"][:, t_], in0=S["h"][:, t_],
                scalar1=S["st"][:, 1, t_:t_ + 1], scalar2=S["st"][:, 2, t_:t_ + 1],
                op0=ALU.mult, op1=ALU.add))(t),
                reads=(f"h{t}", f"st1_{t}", f"st2_{t}"), writes=(f"h{t}",), tag=f"pln{t}")

    def _store_h_as_out(self):
        c, S, Dr = self.c, self.S, self.Dr
        NT = self.cfg["NT"]
        ot = Dr["out"].ap().rearrange("(a p) n -> p a n", p=128)
        for t in range(NT):
            c.dma("pool", (lambda t_: lambda g: g.dma_start(
                out=ot[:, t_], in_=S["h"][:, t_]))(t),
                reads=(f"h{t}",), stream="o", tag=f"sto{t}")

    def _inner_layer(self, stage, li):
        self._inner_attn(stage, li)
        self._inner_ffn(stage, li)

    def _pidparts(self):
        if "pid" not in self._sv:
            self._sv["pid"] = self.nc.partition_id()
            self._sv["rc"] = self._sv["pid"] % 4
            self._sv["bid"] = self._sv["pid"] // 4
            self._sv["rcm1"] = (self._sv["pid"] % 4 + 3) % 4

    def _rc_offset(self, mult, add=0):
        key = ("rcm1", mult, add)
        if key not in self._sv:
            self._pidparts()
            self._sv[key] = (self._sv["bid"] * 4 + self._sv["rcm1"]) * mult + add
        return self._sv[key]

    def _rc_mult(self, mult, add=0, bmult=0):
        key = ("rc", mult, add, bmult)
        if key not in self._sv:
            self._pidparts()
            v = self._sv["rc"] * mult + add
            if bmult:
                v = self._sv["bid"] * bmult + v
            self._sv[key] = v
        return self._sv[key]

    def _inner_attn(self, stage, li):
        cfg, c, S, Dr, nc = self.cfg, self.c, self.S, self.Dr, self.nc
        NT, G = cfg["NT"], cfg["G"]
        L = f"{stage}{li}_"
        c.fence(tag=L + "fat")
        PS = self.psum("PS", [128, 8, 512], F32)
        pb = lambda i: PS[:, i, :]
        groups = [list(range(cfg["n_cores"]))]

        wq = Dr["L0_ffn_Wqkv"].ap()[li].rearrange("(c p) n -> p c n", p=128)
        for half in range(2):
            c.dma("pool", (lambda h_: lambda g: g.dma_start(
                out=S["at_Wqkv"][:, :, h_ * 1536:(h_ + 1) * 1536],
                in_=wq[:, :, h_ * 1536:(h_ + 1) * 1536]))(half),
                writes=("Wqkv",), stream="w", tag=L + f"ldWqkv{half}")
        wo = Dr["L0_ffn_Wo"].ap()[li].rearrange("(c p) n -> p c n", p=128)
        c.dma("pool", lambda g: g.dma_start(out=S["at_Wo"][:, :, :], in_=wo),
              writes=("Wo",), stream="w", tag=L + "ldWo")
        c.dma("pool", lambda g: g.dma_start(
            out=S["sml1b"][0:1, 0:64], in_=Dr["L0_ffn_q_scale"].ap()[li:li + 1, :]),
            writes=("sml1b",), stream="w", tag=L + "ldqs")
        c.op("pe", lambda t: t.matmul(pb(7)[:, 0:64], S["ones1"][:, :], S["sml1b"][0:1, 0:64],
                                      start=True, stop=True),
             reads=("sml1b", "ones1"), writes=("pb7",), tag=L + "qsbc")
        c.op("dve", lambda v: v.tensor_copy(out=S["qsrep"][:, :], in_=pb(7)[:, 0:64]),
             reads=("pb7",), writes=("qsrep",), tag=L + "qsev")
        c.dma("pool", lambda g: g.dma_start(
            out=S["sml1b"][0:1, 0:64], in_=Dr["L0_ffn_k_scale"].ap()[li:li + 1, :]),
            writes=("sml1b",), stream="w", tag=L + "ldks")
        c.op("pe", lambda t: t.matmul(pb(7)[:, 64:128], S["ones1"][:, :], S["sml1b"][0:1, 0:64],
                                      start=True, stop=True),
             reads=("sml1b", "ones1"), writes=("pb7",), tag=L + "ksbc")
        c.op("dve", lambda v: v.tensor_copy(out=S["ksrep"][:, :], in_=pb(7)[:, 64:128]),
             reads=("pb7",), writes=("ksrep",), tag=L + "ksev")
        for nm, base in (("qsrep", 0), ("ksrep", 64)):
            c.op("dve", (lambda n_, b_: lambda v: v.tensor_scalar(
                out=S["scr1k"][:, b_:b_ + 32], in0=S[n_][:, 32:64],
                scalar1=-1.0, scalar2=None, op0=ALU.mult))(nm, base),
                reads=(nm,), writes=("scr1k",), tag=L + f"rot1{nm}")
            c.op("dve", (lambda n_, b_: lambda v: v.tensor_copy(
                out=S["scr1k"][:, b_ + 32:b_ + 64], in_=S[n_][:, 0:32]))(nm, base),
                reads=(nm,), writes=("scr1k",), tag=L + f"rot2{nm}")
        for tt in range(NT + 2):
            for dst, srct, mul, mkey in (("at_fq", "qcos", S["qsrep"][:, :], "qsrep"),
                                         ("at_fq2", "qsin", S["scr1k"][:, 0:64], "scr1k"),
                                         ("at_fk", "kcos", S["ksrep"][:, :], "ksrep"),
                                         ("at_fk2", "ksin", S["scr1k"][:, 64:128], "scr1k")):
                c.op("dve", (lambda d_, s_, m_, t_: lambda v: v.tensor_mul(
                    out=S[d_][:, t_, :], in0=S[s_][:, t_, :], in1=m_))(dst, srct, mul, tt),
                    reads=(srct, mkey), writes=(dst,), tag=L + f"ft{tt}{dst}")

        if self.sub == "w":
            return
        def cast_hb(t):
            c.op("dve", (lambda t_: lambda v: v.tensor_copy(
                out=S["at_hb"][:, t_ % 2, :], in_=S["h"][:, t_, :]))(t),
                reads=(f"h{t}",), writes=(f"hb{t % 2}",), tag=L + f"cast{t}")

        def ut_mms(t):
            hbk = f"hb{t % 2}"
            if t == 0:
                pkey, pap = "hprev", S["hprev"]
            else:
                pkey, pap = f"hb{(t - 1) % 2}", S["at_hb"][:, (t - 1) % 2, :]
            for ch in range(8):
                bank, col = divmod(ch * 128, 512)
                outap = PS[:, bank, col:col + 128]
                if ch < 4:
                    c.op("pe", (lambda t_, c_, o_: lambda te: te.matmul(
                        o_, S["at_hb"][:, t_ % 2, c_ * 128:(c_ + 1) * 128],
                        S["ident"][:, :], start=True, stop=True))(t, ch, outap),
                        reads=(hbk, "ident"), writes=(f"pb{bank}",), tag=L + f"ut{t}_{ch}")
                else:
                    c.op("pe", (lambda t_, c_, o_: lambda te: te.matmul(
                        o_, S["at_hb"][:, t_ % 2, c_ * 128:(c_ + 1) * 128],
                        S["shift"][:, :], start=True, stop=False))(t, ch, outap),
                        reads=(hbk, "shift"), writes=(f"pb{bank}",), tag=L + f"ut{t}_{ch}")
                    c.op("pe", (lambda c_, o_, pa_: lambda te: te.matmul(
                        o_, pa_[:, c_ * 128:(c_ + 1) * 128],
                        S["elast"][:, :], start=False, stop=True))(ch, outap, pap),
                        reads=(pkey, "elast"), writes=(f"pb{bank}",), tag=L + f"ue{t}_{ch}")
            c.op("act", (lambda t_: lambda s: s.copy(
                out=S["at_ubT"][:, 0, :],
                in_=PS[:, 0:2, :].rearrange("p a n -> p (a n)")))(t),
                reads=("pb0", "pb1"), writes=("ubT",), tag=L + f"uev{t}")

        for t in (NT - 3, NT - 2, NT - 1):
            cast_hb(t)

        def contrib(j):
            c.dma("pool", (lambda j_: lambda g: g.dma_start(
                out=bass.AP(Dr["ag_u_in"], j_ * 128, [[256, 128], [128 * 256, 8], [1, 128]]),
                in_=S["at_ubT"][:, 0, :].rearrange("p (c n) -> p c n", n=128)))(j),
                reads=("ubT",), writes=("ag_u_in",), stream="g", tag=L + f"agw{j}")
        ut_mms(NT - 2)
        contrib(0)
        ut_mms(NT - 1)
        contrib(1)
        c.dma("pool", (lambda: lambda g: g.dma_start(
            out=bass.AP(Dr["ag_u_in"], 1024 * 256, [[1, 1], [256, 4], [1, 256]]),
            in_=S["h"][127:128, NT - 1, :].rearrange("q (c n) -> q c n", n=256)))(),
            reads=(f"h{NT - 1}",), writes=("ag_u_in",), stream="g", tag=L + "agwh")
        c.coll(lambda g: g.collective_compute(
            "AllGather", ALU.bypass, replica_groups=groups,
            ins=[Dr["ag_u_in"].ap().opt()], outs=[Dr["ag_u_out"].ap().opt()]),
            reads=("ag_u_in",), writes=("ag_u_out",), tag=L + "agu")

        stride = 1028 * 256
        for j in range(2):
            offj = self._rc_offset(stride, j * 128)
            c.dma("pool", (lambda j_, o_: lambda g: g.dma_start(
                out=S["at_uh"][:, j_, :].rearrange("p (c n) -> p c n", n=128),
                in_=bass.AP(Dr["ag_u_out"], o_, [[256, 128], [128 * 256, 8], [1, 128]])))(j, offj),
                reads=("ag_u_out",), writes=("uh",), stream="g", tag=L + f"agr{j}")
        offh = self._rc_offset(stride, 1024 * 256)
        c.dma("pool", (lambda o_: lambda g: g.dma_start(
            out=S["hprev"][127:128, :].rearrange("q (c n) -> q c n", n=256),
            in_=bass.AP(Dr["ag_u_out"], o_, [[1, 1], [256, 4], [1, 256]])))(offh),
            reads=("ag_u_out",), writes=("hprev",), stream="g", tag=L + "agrh")
        c.op("dve", lambda v: v.tensor_scalar_mul(
            out=S["at_uh"][:, :, :], in0=S["at_uh"][:, :, :], scalar1=S["hblr"][:, :]),
            reads=("uh", "hblr"), writes=("uh",), tag=L + "ublend")
        c.op("dve", lambda v: v.tensor_scalar_mul(
            out=S["hprev"][:, :], in0=S["hprev"][:, :], scalar1=S["hblr"][:, :]),
            reads=("hprev", "hblr"), writes=("hprev",), tag=L + "hblend")
        if self.sub == "ag":
            return

        def qkv_tile(t):
            own = t >= 0
            ub = S["at_ubT"][:, 0, :] if own else S["at_uh"][:, t + 2, :]
            ubk = ("ubT",) if own else ("uh",)
            tt = t + 2
            passes = ([("q", 0)] if own else []) + [("k", 1), ("v", 2)]
            for nm, gi in passes:
                for half in range(2):
                    bank = 2 + half
                    for kc in range(8):
                        c.op("pe", (lambda g_, h_, k_, b_, u_: lambda te: te.matmul(
                            pb(b_), u_[:, k_ * 128:(k_ + 1) * 128],
                            S["at_Wqkv"][:, k_, g_ * 1024 + h_ * 512:g_ * 1024 + h_ * 512 + 512],
                            start=(k_ == 0), stop=(k_ == 7)))(gi, half, kc, bank, ub),
                            reads=ubk + ("Wqkv",), writes=(f"pb{bank}",),
                            tag=L + f"qkv{t}_{nm}{half}_{kc}")
                if nm == "v":
                    c.op("act", (lambda t_: lambda s: s.copy(
                        out=S["at_vT"][:, t_ % 3, :],
                        in_=PS[:, 2:4, :].rearrange("p a n -> p (a n)")))(t),
                        reads=("pb2", "pb3"), writes=(f"vT{t % 3}",), tag=L + f"vev{t}")
                    continue
                cbase = 0 if nm == "q" else 16
                for half in range(2):
                    c.op("act", (lambda h_: lambda s: s.activation(
                        out=S["scr1k"][:, 0:512], in_=pb(2 + h_), func=AF.Square))(half),
                        reads=(f"pb{2 + half}",), writes=("scr1k",), tag=L + f"sq{t}{nm}{half}")
                    c.op("dve", (lambda h_, cb_: lambda v: v.reduce_sum(
                        out=S["inv"][:, cb_ + h_ * 8:cb_ + h_ * 8 + 8],
                        in_=S["scr1k"][:, 0:512].rearrange("p (s n) -> p s n", n=64),
                        axis=AX))(half, cbase),
                        reads=("scr1k",), writes=("inv",), tag=L + f"ssr{t}{nm}{half}")
                kcl = self.kk(1e-24); kmh = self.kk(-0.5)
                c.op("dve", (lambda cb_: lambda v: v.tensor_scalar_max(
                    out=S["inv"][:, cb_:cb_ + 16], in0=S["inv"][:, cb_:cb_ + 16],
                    scalar1=1e-24))(cbase),
                    reads=("inv",), writes=("inv",), tag=L + f"cl{t}{nm}")
                c.op("act", (lambda cb_: lambda s: s.activation(
                    out=S["inv"][:, cb_:cb_ + 16], in_=S["inv"][:, cb_:cb_ + 16],
                    func=AF.Ln))(cbase),
                    reads=("inv",), writes=("inv",), tag=L + f"li{t}{nm}")
                c.op("act", (lambda cb_, k_: lambda s: s.activation(
                    out=S["inv"][:, cb_:cb_ + 16], in_=S["inv"][:, cb_:cb_ + 16],
                    func=AF.Exp, scale=k_))(cbase, kmh),
                    reads=("inv", "konst"), writes=("inv",), tag=L + f"ei{t}{nm}")
                dst = "qb" if nm == "q" else "kb"
                for half in range(2):
                    c.op("dve", (lambda h_, cb_, d_: lambda v: v.tensor_mul(
                        out=S["at_" + d_][:, h_ * 512:(h_ + 1) * 512].rearrange(
                            "p (s n) -> p s n", n=64),
                        in0=pb(2 + h_).rearrange("p (s n) -> p s n", n=64),
                        in1=S["inv"][:, cb_ + h_ * 8:cb_ + h_ * 8 + 8].unsqueeze(2).broadcast_to([128, 8, 64])))(half, cbase, dst),
                        reads=(f"pb{2 + half}", "inv"), writes=(dst,), tag=L + f"ia{t}{nm}{half}")
                eng = "dve" if nm == "q" else "pool"
                ft, ft2 = ("at_fq", "at_fq2") if nm == "q" else ("at_fk", "at_fk2")
                rtn = "rtmp" if nm == "q" else "rtmp2"
                src = S["at_" + dst][:, :].rearrange("p (s n) -> p s n", n=64)
                rt = S["at_" + rtn][:, :].rearrange("p (s n) -> p s n", n=64)
                c.op(eng, (lambda s_, r_, f_, t_: lambda v: v.tensor_mul(
                    out=r_[:, :, 0:32], in0=s_[:, :, 32:64],
                    in1=S[f_][:, t_, 0:32].unsqueeze(1).broadcast_to([128, 16, 32])))(src, rt, ft2, tt),
                    reads=(dst, ft2), writes=(rtn,), tag=L + f"rA{t}{nm}")
                c.op(eng, (lambda s_, r_, f_, t_: lambda v: v.tensor_mul(
                    out=r_[:, :, 32:64], in0=s_[:, :, 0:32],
                    in1=S[f_][:, t_, 32:64].unsqueeze(1).broadcast_to([128, 16, 32])))(src, rt, ft2, tt),
                    reads=(dst, ft2), writes=(rtn,), tag=L + f"rB{t}{nm}")
                c.op(eng, (lambda s_, f_, t_: lambda v: v.tensor_mul(
                    out=s_, in0=s_,
                    in1=S[f_][:, t_, :].unsqueeze(1).broadcast_to([128, 16, 64])))(src, ft, tt),
                    reads=(dst, ft), writes=(dst,), tag=L + f"rC{t}{nm}")
                c.op(eng, (lambda s_, r_: lambda v: v.tensor_add(
                    out=s_, in0=s_, in1=r_))(src, rt),
                    reads=(dst, rtn), writes=(dst,), tag=L + f"rD{t}{nm}")
                tdst = ("qT", 0) if nm == "q" else ("kT", t % 3)
                for ch in range(8):
                    bank, col = divmod(ch * 128, 512)
                    c.op("pe", (lambda c_, b_, co_, d_: lambda te: te.matmul(
                        PS[:, b_, co_:co_ + 128],
                        S["at_" + d_][:, c_ * 128:(c_ + 1) * 128],
                        S["ident"][:, :], start=True, stop=True))(ch, bank, col, dst),
                        reads=(dst, "ident"), writes=(f"pb{bank}",), tag=L + f"tr{t}{nm}{ch}")
                c.op("act", (lambda td_: lambda s: s.copy(
                    out=S["at_" + td_[0]][:, td_[1], :],
                    in_=PS[:, 0:2, :].rearrange("p a n -> p (a n)")))(tdst),
                    reads=("pb0", "pb1"), writes=(f"{tdst[0]}{tdst[1]}",), tag=L + f"tev{t}{nm}")

        qkv_tile(-2)
        qkv_tile(-1)
        if self.sub == "qkvpre":
            return

        def attn_tile(t):
            kts = [t - 2, t - 1, t]
            for h in range(NH):
                sb = 5 + h % 2
                qT = S["at_qT"][:, 0, :]
                for ji, j in enumerate(kts):
                    c.op("pe", (lambda h_, j_, ji_, sb_: lambda te: te.matmul(
                        pb(sb_)[:, ji_ * 128:ji_ * 128 + 128],
                        S["at_qT"][(h_ % 2) * 64:(h_ % 2) * 64 + 64, 0,
                                   (h_ // 2) * 128:(h_ // 2) * 128 + 128],
                        S["at_kT"][(h_ % 2) * 64:(h_ % 2) * 64 + 64, j_ % 3,
                                   (h_ // 2) * 128:(h_ // 2) * 128 + 128],
                        start=True, stop=True))(h, j, ji, sb),
                        reads=("qT0", f"kT{j % 3}"), writes=(f"pb{sb}",),
                        tag=L + f"sc{t}h{h}j{ji}")
                c.op("dve", (lambda h_, sb_: lambda v: v.reduce_max(
                    out=S["sml"][:, h_:h_ + 1], in_=pb(sb_)[:, 0:384], axis=AX,
                    negate=True))(h, sb),
                    reads=(f"pb{sb}",), writes=(f"nmx{h % 2}",), tag=L + f"mx{t}h{h}")
                c.op("act", (lambda h_, sb_: lambda s: s.activation(
                    out=S["at_P"][:, h_ % 2, :], in_=pb(sb_)[:, 0:384], func=AF.Exp,
                    bias=S["sml"][:, h_:h_ + 1]))(h, sb),
                    reads=(f"pb{sb}", f"nmx{h % 2}"), writes=(f"P{h % 2}",), tag=L + f"ex{t}h{h}")
                if self.sub == "sc" and h == 0:
                    return True
                if t < 2:
                    c.op("pool", (lambda h_, t_: lambda g: g.tensor_mul(
                        out=S["at_P"][:, h_ % 2, :], in0=S["at_P"][:, h_ % 2, :],
                        in1=S["bmask"][:, t_, :]))(h, t),
                        reads=(f"P{h % 2}", "bmask"), writes=(f"P{h % 2}",),
                        tag=L + f"mk{t}h{h}")
                else:
                    c.op("pool", (lambda h_: lambda g: g.tensor_mul(
                        out=S["at_P"][:, h_ % 2, 0:128], in0=S["at_P"][:, h_ % 2, 0:128],
                        in1=S["band"][:, 0:128]))(h),
                        reads=(f"P{h % 2}", "band"), writes=(f"P{h % 2}",),
                        tag=L + f"mk0{t}h{h}")
                    c.op("pool", (lambda h_: lambda g: g.tensor_mul(
                        out=S["at_P"][:, h_ % 2, 256:384], in0=S["at_P"][:, h_ % 2, 256:384],
                        in1=S["band"][:, 256:384]))(h),
                        reads=(f"P{h % 2}", "band"), writes=(f"P{h % 2}",),
                        tag=L + f"mk2{t}h{h}")
                c.op("dve", (lambda h_: lambda v: v.reduce_sum(
                    out=S["rden"][:, h_:h_ + 1], in_=S["at_P"][:, h_ % 2, :],
                    axis=AX))(h),
                    reads=(f"P{h % 2}",), writes=("rden",), tag=L + f"ds{t}h{h}")
                c.op("dve", (lambda h_: lambda v: v.reciprocal(
                    out=S["sml"][:, 16 + h_:17 + h_], in_=S["rden"][:, h_:h_ + 1]))(h),
                    reads=("rden",), writes=(f"rd{h % 2}",), tag=L + f"dr{t}h{h}")
                c.op("dve", (lambda h_: lambda v: v.tensor_scalar_mul(
                    out=S["at_P"][:, h_ % 2, :], in0=S["at_P"][:, h_ % 2, :],
                    scalar1=S["sml"][:, 16 + h_:17 + h_]))(h),
                    reads=(f"P{h % 2}", f"rd{h % 2}"), writes=(f"P{h % 2}",),
                    tag=L + f"dm{t}h{h}")
                for ji in range(3):
                    c.op("pe", (lambda h_, ji_: lambda te: te.matmul(
                        PS[:, 0, ji_ * 128:ji_ * 128 + 128],
                        S["at_P"][:, h_ % 2, ji_ * 128:ji_ * 128 + 128],
                        S["ident"][:, :], start=True, stop=True))(h, ji),
                        reads=(f"P{h % 2}", "ident"), writes=("pb0",),
                        tag=L + f"pt{t}h{h}j{ji}")
                c.op("act", (lambda h_: lambda s: s.copy(
                    out=S["at_PT"][:, h_ % 2, :, :],
                    in_=PS[:, 0, 0:384].rearrange("p (j n) -> p j n", n=128)))(h),
                    reads=("pb0",), writes=(f"PT{h % 2}",), tag=L + f"ptev{t}h{h}")
                reg = (h // 4) % 2
                for ji in range(3):
                    c.op("pe", (lambda h_, ji_, j_, reg_: lambda te: te.matmul(
                        pb(4)[(h_ % 2) * 64:(h_ % 2) * 64 + 64,
                              reg_ * 256 + ((h_ // 2) % 2) * 128:reg_ * 256 + ((h_ // 2) % 2) * 128 + 128],
                        S["at_vT"][:, j_ % 3, h_ * 64:h_ * 64 + 64],
                        S["at_PT"][:, h_ % 2, ji_, :],
                        start=(ji_ == 0), stop=(ji_ == 2)))(h, ji, kts[ji], reg),
                        reads=(f"PT{h % 2}", f"vT{kts[ji] % 3}"), writes=(f"pb4r{reg}",),
                        tag=L + f"av{t}h{h}j{ji}")
                if h % 4 == 3:
                    w = h // 4
                    c.op("act", (lambda w_, reg_: lambda s: s.copy(
                        out=S["at_attT"][:, w_ * 2:(w_ + 1) * 2, :].rearrange("p a n -> p (a n)"),
                        in_=pb(4)[:, reg_ * 256:reg_ * 256 + 256]))(w, reg),
                        reads=(f"pb4r{reg}",), writes=("attT",), tag=L + f"ave{t}w{w}")
                if self.sub == "av" and h == 3:
                    return True
            if self.sub == "rcp":
                return True
            for gi in range(2):
                for ch in range(8):
                    c.op("pe", (lambda g_, c_: lambda te: te.matmul(
                        pb(2 + g_), S["at_attT"][:, c_, :],
                        S["at_Wo"][:, c_, g_ * 512:(g_ + 1) * 512],
                        start=(c_ == 0), stop=(c_ == 7)))(gi, ch),
                        reads=("attT", "Wo"), writes=(f"pb{2 + gi}",), tag=L + f"wo{t}{gi}{ch}")
                c.op("dve", (lambda t_, g_: lambda v: v.tensor_add(
                    out=S["h"][:, t_, g_ * 512:(g_ + 1) * 512],
                    in0=pb(2 + g_), in1=S["h"][:, t_, g_ * 512:(g_ + 1) * 512]))(t, gi),
                    reads=(f"pb{2 + gi}", f"h{t}"), writes=(f"h{t}",), tag=L + f"res{t}{gi}")

        for t in range(NT):
            cast_hb(t)
            ut_mms(t)
            qkv_tile(t)
            if self.sub == "qkv0" and t == 0:
                return
            if attn_tile(t):
                return
            if self.sub == "tile0" and t == 0:
                return
        agh = Dr["ag_hl_in"].ap()
        c.dma("pool", lambda g: g.dma_start(
            out=bass.AP(Dr["ag_hl_in"], 0, [[1, 1], [128, 8], [1, 128]]),
            in_=S["h"][127:128, NT - 1, :].rearrange("q (c n) -> q c n", n=128)),
            reads=(f"h{NT - 1}",), writes=("ag_hl_in",), stream="g", tag=L + "aghw")
        c.coll(lambda g: g.collective_compute(
            "AllGather", ALU.bypass, replica_groups=groups,
            ins=[Dr["ag_hl_in"].ap().opt()], outs=[Dr["ag_hl_out"].ap().opt()]),
            reads=("ag_hl_in",), writes=("ag_hl_out",), tag=L + "agh")
        off2 = self._rc_offset(8 * 128)
        c.dma("pool", lambda g: g.dma_start(
            out=S["hlast"][0:1, :].rearrange("q (c n) -> q c n", n=128),
            in_=bass.AP(Dr["ag_hl_out"], off2, [[1, 1], [128, 8], [1, 128]])),
            reads=("ag_hl_out",), writes=("hlast",), stream="g", tag=L + "aghr")
        c.op("dve", lambda v: v.tensor_scalar_mul(
            out=S["hlast"][0:1, :], in0=S["hlast"][0:1, :], scalar1=S["hblr"][0:1, :]),
            reads=("hlast", "hblr"), writes=("hlast",), tag=L + "hlbl")

    def _inner_ffn(self, stage, li):
        cfg, c, S, Dr, nc = self.cfg, self.c, self.S, self.Dr, self.nc
        NT = cfg["NT"]
        L = f"{stage}{li}f_"
        PS = self.psum("PS", [128, 8, 512], F32)
        pb = lambda i: PS[:, i, :]
        st = S["st"]
        c.fence(tag=L + "fff")

        c.dma("pool", lambda g: g.dma_start(
            out=S["bscr"][0:1, 0:D], in_=Dr["L0_ffn_ln_g"].ap()[li:li + 1, :]),
            writes=("bscr",), stream="w", tag=L + "ldg")
        for half in range(2):
            c.op("pe", (lambda h_: lambda te: te.matmul(
                pb(7)[:, :], S["ones1"][:, :], S["bscr"][0:1, h_ * 512:(h_ + 1) * 512],
                start=True, stop=True))(half),
                reads=("ones1", "bscr"), writes=("pb7",), tag=L + f"gbc{half}")
            c.op("dve", (lambda h_: lambda v: v.tensor_copy(
                out=S["ff_grep"][:, h_ * 512:(h_ + 1) * 512], in_=pb(7)[:, :]))(half),
                reads=("pb7",), writes=("grep",), tag=L + f"gev{half}")
        def bias_hilo(src_ap, width, dstkey, dst):
            for i0 in range(0, width, 512):
                w_ = min(512, width - i0)
                c.dma("pool", (lambda i_, w2, sa: lambda g: g.dma_start(
                    out=S["scr1k"][0:1, 0:w2], in_=sa[0:1, i_:i_ + w2]))(i0, w_, src_ap),
                    writes=("scr1k",), stream="w", tag=L + f"ldb_{dstkey}{i0}")
                c.op("dve", (lambda i_, w2: lambda v: v.tensor_copy(
                    out=dst[0:1, i_:i_ + w2], in_=S["scr1k"][0:1, 0:w2]))(i0, w_),
                    reads=("scr1k",), writes=(dstkey,), tag=L + f"bhi_{dstkey}{i0}")
                c.op("dve", (lambda i_, w2: lambda v: v.tensor_sub(
                    out=S["bscr"][0:1, 0:w2], in0=S["scr1k"][0:1, 0:w2],
                    in1=dst[0:1, i_:i_ + w2]))(i0, w_),
                    reads=("scr1k", dstkey), writes=("bscr",), tag=L + f"blo_{dstkey}{i0}")
                c.dma("sp", (lambda i_, w2: lambda g: g.dma_start(
                    out=dst[1:2, i_:i_ + w2], in_=S["bscr"][0:1, 0:w2]))(i0, w_),
                    reads=("bscr",), writes=(dstkey,), stream="s", tag=L + f"blod_{dstkey}{i0}")
        bias_hilo(Dr["L0_ffn_b1"].ap()[li:li + 1, :], 2 * DI, "b1hl", S["ff_b1hl"])
        bias_hilo(Dr["L0_ffn_b2"].ap()[li:li + 1, :], D, "b2hl", S["ff_b2hl"])
        bias_hilo(Dr["L0_ffn_ln_b"].ap()[li:li + 1, :], D, "blnhl", S["ff_blnhl"])
        w2 = Dr["L0_ffn_W2"].ap()[li]
        c.dma("pool", lambda g: g.dma_start(
            out=S["ff_W2"][:, 0:21, :],
            in_=w2[0:2688, :].rearrange("(c p) n -> p c n", p=128)),
            writes=("W2",), stream="w", tag=L + "ldW2a")
        c.dma("pool", lambda g: g.dma_start(
            out=S["ff_W2"][0:42, 21, :], in_=w2[2688:2730, :]),
            writes=("W2",), stream="w", tag=L + "ldW2b")

        for t in range(NT):
            c.op("dve", (lambda t_: lambda v: v.reduce_sum(
                out=st[:, 4:6, t_],
                in_=S["h"][:, t_, :].rearrange("p (a n) -> p a n", n=512), axis=AX))(t),
                reads=(f"h{t}",), writes=("stS",), tag=L + f"s12_{t}")
            c.op("act", (lambda t_: lambda s: s.activation(
                out=S["scr1k"][:, :], in_=S["h"][:, t_, :], func=AF.Square))(t),
                reads=(f"h{t}",), writes=("scr1k",), tag=L + f"sq_{t}")
            c.op("dve", (lambda t_: lambda v: v.reduce_sum(
                out=st[:, 6:8, t_],
                in_=S["scr1k"][:, :].rearrange("p (a n) -> p a n", n=512), axis=AX))(t),
                reads=("scr1k",), writes=("stS",), tag=L + f"q12_{t}")
        for (srcc, dstc) in ((5, 8), (7, 9)):
            c.dma("sp", (lambda s_, d_: lambda g: g.dma_start(
                out=st[1:128, d_, 0:NT], in_=st[0:127, s_, 0:NT]))(srcc, dstc),
                reads=("stS",), writes=("stSH",), stream="s", tag=L + f"sh{srcc}")
            if NT > 1:
                c.dma("sp", (lambda s_, d_: lambda g: g.dma_start(
                    out=st[0:1, d_, 1:NT], in_=st[127:128, s_, 0:NT - 1]))(srcc, dstc),
                    reads=("stS",), writes=("stSH",), stream="s", tag=L + f"shb{srcc}")
        c.op("dve", lambda v: v.reduce_sum(
            out=st[0:1, 8, 0:1], in_=S["hlast"][0:1, 512:1024], axis=AX),
            reads=("hlast",), writes=("stSH",), tag=L + "hls")
        c.op("act", lambda s: s.activation(
            out=S["scr1k"][0:1, 0:512], in_=S["hlast"][0:1, 512:1024], func=AF.Square,
            accum_out=st[0:1, 9, 0:1]),
            reads=("hlast",), writes=("stSH", "scr1k"), tag=L + "hlq")
        c.op("dve", lambda v: v.tensor_add(
            out=st[:, 10, 0:NT], in0=st[:, 4, 0:NT], in1=st[:, 8, 0:NT]),
            reads=("stS", "stSH"), writes=("stC",), tag=L + "cS")
        c.op("dve", lambda v: v.tensor_add(
            out=st[:, 11, 0:NT], in0=st[:, 6, 0:NT], in1=st[:, 9, 0:NT]),
            reads=("stS", "stSH"), writes=("stC",), tag=L + "cQ")
        c.op("dve", lambda v: v.tensor_scalar(
            out=st[:, 12, 0:NT], in0=st[:, 10, 0:NT], scalar1=1.0 / D, scalar2=None,
            op0=ALU.mult), reads=("stC",), writes=("stM",), tag=L + "mu")
        c.op("dve", lambda v: v.scalar_tensor_tensor(
            out=st[:, 13, 0:NT], in0=st[:, 12, 0:NT], scalar=-1.0, in1=st[:, 12, 0:NT],
            op0=ALU.mult, op1=ALU.mult), reads=("stM",), writes=("stV",), tag=L + "m2")
        c.op("dve", lambda v: v.scalar_tensor_tensor(
            out=st[:, 13, 0:NT], in0=st[:, 11, 0:NT], scalar=1.0 / D, in1=st[:, 13, 0:NT],
            op0=ALU.mult, op1=ALU.add), reads=("stC", "stV"), writes=("stV",), tag=L + "var")
        c.op("dve", lambda v: v.tensor_scalar_max(
            out=st[:, 13, 0:NT], in0=st[:, 13, 0:NT], scalar1=0.0),
            reads=("stV",), writes=("stV",), tag=L + "cl")
        keps = self.kk(1e-5); kmh = self.kk(-0.5)
        c.op("act", (lambda k_: lambda s: s.activation(
            out=st[:, 13, 0:NT], in_=st[:, 13, 0:NT], func=AF.Ln, bias=k_))(keps),
            reads=("stV", "konst"), writes=("stV",), tag=L + "ln")
        c.op("act", (lambda k_: lambda s: s.activation(
            out=st[:, 13, 0:NT], in_=st[:, 13, 0:NT], func=AF.Exp, scale=k_))(kmh),
            reads=("stV", "konst"), writes=("stV",), tag=L + "rstd")
        c.op("dve", lambda v: v.scalar_tensor_tensor(
            out=st[:, 14, 0:NT], in0=st[:, 12, 0:NT], scalar=-1.0, in1=st[:, 13, 0:NT],
            op0=ALU.mult, op1=ALU.mult), reads=("stM", "stV"), writes=("stN",), tag=L + "nmr")
        for (srcc, dstc, key) in ((13, 12, "stVp"), (14, 15, "stNp")):
            c.dma("sp", (lambda s_, d_: lambda g: g.dma_start(
                out=st[0:127, d_, 0:NT], in_=st[1:128, s_, 0:NT]))(srcc, dstc),
                reads=("stV", "stN"), writes=(key,), stream="s", tag=L + f"pu{srcc}")
            if NT > 1:
                c.dma("sp", (lambda s_, d_: lambda g: g.dma_start(
                    out=st[127:128, d_, 0:NT - 1], in_=st[0:1, s_, 1:NT]))(srcc, dstc),
                    reads=("stV", "stN"), writes=(key,), stream="s", tag=L + f"pub{srcc}")
            c.dma("sp", (lambda s_, d_: lambda g: g.dma_start(
                out=st[127:128, d_, NT - 1:NT], in_=st[127:128, s_, NT - 1:NT]))(srcc, dstc),
                reads=("stV", "stN"), writes=(key,), stream="s", tag=L + f"puc{srcc}")

        c.op("pool", lambda g: g.memset(S["hprev"][:, :], 0.0),
             writes=("hprev",), tag=L + "zxhl")
        c.op("dve", lambda v: v.tensor_scalar(
            out=S["ff_xa"][0:1, 0, 512:1024], in0=S["hlast"][0:1, 512:1024],
            scalar1=st[0:1, 13, 0:1], scalar2=st[0:1, 14, 0:1],
            op0=ALU.mult, op1=ALU.add),
            reads=("hlast", "stV", "stN"), writes=("xa0",), tag=L + "xhl1")
        c.op("dve", lambda v: v.tensor_mul(
            out=S["ff_xa"][0:1, 0, 512:1024], in0=S["ff_xa"][0:1, 0, 512:1024],
            in1=S["ff_grep"][0:1, 512:1024]),
            reads=("xa0", "grep"), writes=("xa0",), tag=L + "xhl2")
        c.dma("sp", lambda g: g.dma_start(
            out=S["hprev"][127:128, 512:1024], in_=S["ff_xa"][0:1, 0, 512:1024]),
            reads=("xa0",), writes=("hprev",), stream="s", tag=L + "xhl3")

        NQ = NT // 4
        kts = [512] * 5 + [170]
        for qq in range(NQ):
            for tl in range(4):
                t = qq * 4 + tl
                sl = t % 2
                c.op("dve", (lambda t_, s_: lambda v: v.tensor_scalar(
                    out=S["ff_xa"][:, s_, 0:512], in0=S["h"][:, t_, 0:512],
                    scalar1=st[:, 13, t_:t_ + 1], scalar2=st[:, 14, t_:t_ + 1],
                    op0=ALU.mult, op1=ALU.add))(t, sl),
                    reads=(f"h{t}", "stV", "stN"), writes=(f"xa{sl}",), tag=L + f"xa1_{t}")
                c.op("dve", (lambda t_, s_: lambda v: v.tensor_scalar(
                    out=S["ff_xa"][:, s_, 512:1024], in0=S["h"][:, t_, 512:1024],
                    scalar1=st[:, 12, t_:t_ + 1], scalar2=st[:, 15, t_:t_ + 1],
                    op0=ALU.mult, op1=ALU.add))(t, sl),
                    reads=(f"h{t}", "stVp", "stNp"), writes=(f"xa{sl}",), tag=L + f"xa2_{t}")
                c.op("dve", (lambda s_: lambda v: v.tensor_mul(
                    out=S["ff_xa"][:, s_, :], in0=S["ff_xa"][:, s_, :],
                    in1=S["ff_grep"][:, :]))(sl),
                    reads=(f"xa{sl}", "grep"), writes=(f"xa{sl}",), tag=L + f"xg_{t}")
                if t == 0:
                    pkey, pap = "hprev", S["hprev"]
                else:
                    pkey, pap = f"xa{(t - 1) % 2}", S["ff_xa"][:, (t - 1) % 2, :]
                for ch in range(8):
                    bank, col = divmod(ch * 128, 512)
                    outap = PS[:, bank, col:col + 128]
                    if ch < 4:
                        c.op("pe", (lambda s_, c_, o_: lambda te: te.matmul(
                            o_, S["ff_xa"][:, s_, c_ * 128:(c_ + 1) * 128],
                            S["ident"][:, :], start=True, stop=False))(sl, ch, outap),
                            reads=(f"xa{sl}", "ident"), writes=(f"pb{bank}",),
                            tag=L + f"xt{t}_{ch}")
                    else:
                        c.op("pe", (lambda s_, c_, o_: lambda te: te.matmul(
                            o_, S["ff_xa"][:, s_, c_ * 128:(c_ + 1) * 128],
                            S["shift"][:, :], start=True, stop=False))(sl, ch, outap),
                            reads=(f"xa{sl}", "shift"), writes=(f"pb{bank}",),
                            tag=L + f"xt{t}_{ch}")
                        c.op("pe", (lambda c_, o_, pa_, pk_: lambda te: te.matmul(
                            o_, pa_[:, c_ * 128:(c_ + 1) * 128],
                            S["elast"][:, :], start=False, stop=False))(ch, outap, pap, pkey),
                            reads=(pkey, "elast"), writes=(f"pb{bank}",),
                            tag=L + f"xe{t}_{ch}")
                    c.op("pe", (lambda c_, o_: lambda te: te.matmul(
                        o_, S["ff_blnhl"][0:2, c_ * 128:(c_ + 1) * 128],
                        S["ones2"][:, :], start=False, stop=True))(ch, outap),
                        reads=("blnhl", "ones2"), writes=(f"pb{bank}",),
                        tag=L + f"xb{t}_{ch}")
                c.op("act", (lambda tl_: lambda s: s.copy(
                    out=S["ff_xnT"][:, :, tl_ * 128:tl_ * 128 + 128],
                    in_=PS[:, 0:2, :].rearrange("p a n -> p (a n)").rearrange("p (c n) -> p c n", n=128)))(tl),
                    reads=("pb0", "pb1"), writes=("xnT",), tag=L + f"xev{t}")
            for pg in range(6):
                w = kts[pg]
                buf = "ff_W1"
                bkey = "W1x"
                w1 = Dr["L0_ffn_W1"].ap()[li]
                c.dma("pool", (lambda b_, p_, w_: lambda g: g.dma_start(
                    out=S[b_][:, :, 0:w_],
                    in_=w1[:, p_ * 512:p_ * 512 + w_].rearrange("(c p) n -> p c n", p=128)))(buf, pg, w),
                    writes=(bkey,), stream="w", tag=L + f"q{qq}ldW1a{pg}")
                c.dma("pool", (lambda b_, p_, w_: lambda g: g.dma_start(
                    out=S[b_][:, :, 512:512 + w_],
                    in_=w1[:, DI + p_ * 512:DI + p_ * 512 + w_].rearrange("(c p) n -> p c n", p=128)))(buf, pg, w),
                    writes=(bkey,), stream="w", tag=L + f"q{qq}ldW1g{pg}")
                for tl in range(4):
                    t = qq * 4 + tl
                    for part, bank in ((0, 2), (1, 3)):
                        for kc in range(8):
                            c.op("pe", (lambda b_, p_, k_, bk_, tl_, w_: lambda te: te.matmul(
                                PS[:, bk_, 0:w_],
                                S["ff_xnT"][:, k_, tl_ * 128:tl_ * 128 + 128],
                                S[b_][:, k_, p_ * 512:p_ * 512 + w_],
                                start=(k_ == 0), stop=False))(buf, part, kc, bank, tl, w),
                                reads=(bkey, "xnT"), writes=(f"pb{bank}",),
                                tag=L + f"w1_{t}_{pg}_{part}_{kc}")
                        boff = pg * 512 if part == 0 else DI + pg * 512
                        c.op("pe", (lambda bo_, bk_, w_: lambda te: te.matmul(
                            PS[:, bk_, 0:w_], S["ones2"][:, :],
                            S["ff_b1hl"][0:2, bo_:bo_ + w_], start=False, stop=True))(boff, bank, w),
                            reads=("b1hl", "ones2"), writes=(f"pb{bank}",),
                            tag=L + f"b1_{t}_{pg}_{part}")
                    c.op("act", (lambda w_: lambda s: s.activation(
                        out=S["bscr2"][:, 0:w_], in_=PS[:, 3, 0:w_], func=AF.Gelu))(w),
                        reads=("pb3",), writes=("gelu",), tag=L + f"ge_{t}_{pg}")
                    c.op("dve", (lambda w_: lambda v: v.tensor_mul(
                        out=S["bscr"][:, 0:w_], in0=PS[:, 2, 0:w_],
                        in1=S["bscr2"][:, 0:w_]))(w),
                        reads=("pb2", "gelu"), writes=("h2t",), tag=L + f"h2_{t}_{pg}")
                    ncc = (w + 127) // 128
                    for cc in range(ncc):
                        cw = min(128, w - cc * 128)
                        kr = pg * 4 + cc
                        bank = 0 if cc % 2 == 0 else 1
                        c.op("pe", (lambda c_, cw_, b_: lambda te: te.matmul(
                            PS[0:cw_, b_, 0:128],
                            S["bscr"][:, c_ * 128:c_ * 128 + cw_],
                            S["ident"][:, :], start=True, stop=True))(cc, cw, bank),
                            reads=("h2t", "ident"), writes=(f"pb{bank}",),
                            tag=L + f"ht_{t}_{pg}_{cc}")
                        c.op("act", (lambda k_, cw_, tl_, b_: lambda s: s.copy(
                            out=S["ff_h2T"][0:cw_, k_, tl_ * 128:tl_ * 128 + 128],
                            in_=PS[0:cw_, b_, 0:128]))(kr, cw, tl, bank),
                            reads=(f"pb{bank}",), writes=("h2T",), tag=L + f"he_{t}_{pg}_{cc}")
            for tl in range(4):
                t = qq * 4 + tl
                for og in range(2):
                    bank = 2 + og
                    for kr in range(NKC2):
                        cw = 128 if kr < 21 else 42
                        c.op("pe", (lambda k_, cw_, tl_, b_, o_: lambda te: te.matmul(
                            PS[:, b_, :], S["ff_h2T"][0:cw_, k_, tl_ * 128:tl_ * 128 + 128],
                            S["ff_W2"][0:cw_, k_, o_ * 512:(o_ + 1) * 512],
                            start=(k_ == 0), stop=False))(kr, cw, tl, bank, og),
                            reads=("h2T", "W2"), writes=(f"pb{bank}",),
                            tag=L + f"w2_{t}_{og}_{kr}")
                    c.op("pe", (lambda b_, o_: lambda te: te.matmul(
                        PS[:, b_, :], S["ones2"][:, :],
                        S["ff_b2hl"][0:2, o_ * 512:(o_ + 1) * 512], start=False, stop=True))(bank, og),
                        reads=("b2hl", "ones2"), writes=(f"pb{bank}",),
                        tag=L + f"b2_{t}_{og}")
                    c.op("dve", (lambda t_, b_, o_: lambda v: v.tensor_add(
                        out=S["h"][:, t_, o_ * 512:(o_ + 1) * 512],
                        in0=PS[:, b_, :], in1=S["h"][:, t_, o_ * 512:(o_ + 1) * 512]))(t, bank, og),
                        reads=(f"pb{bank}", f"h{t}"), writes=(f"h{t}",), tag=L + f"r_{t}_{og}")

    def _stage_boundary_B_end(self):
        cfg, c, S, Dr = self.cfg, self.c, self.S, self.Dr
        NT, G = cfg["NT"], cfg["G"]
        L = "bb_"
        PS = self.psum("PS", [128, 8, 512], F32)
        groups = [list(range(cfg["n_cores"]))]
        xt = Dr["x"].ap().rearrange("(a p) n -> p a n", p=128)
        c.fence(tag=L + "f")
        for t in range(NT):
            c.dma("sp", (lambda t_: lambda g: g.dma_start(
                out=S["scr1k"][:, :], in_=xt[:, t_]))(t),
                writes=("scr1k",), stream="x", tag=L + f"ldx{t}")
            c.op("dve", (lambda t_: lambda v: v.scalar_tensor_tensor(
                out=S["h"][:, t_, :], in0=S["h"][:, t_, :], scalar=0.5,
                in1=S["scr1k"][:, :], op0=ALU.mult, op1=ALU.add))(t),
                reads=(f"h{t}", "scr1k"), writes=(f"h{t}", "scr1k_r"), tag=L + f"x1_{t}")
        self._ln_stats_plain(lambda t: f"h{t}", lambda t: S["h"][:, t], 0)
        for t in range(NT):
            c.op("dve", (lambda t_: lambda v: v.scalar_tensor_tensor(
                out=S["st"][:, 2, t_:t_ + 1], in0=S["st"][:, 0, t_:t_ + 1],
                scalar=-1.0, in1=S["st"][:, 1, t_:t_ + 1],
                op0=ALU.mult, op1=ALU.mult))(t),
                reads=(f"st0_{t}", f"st1_{t}"), writes=(f"st2_{t}",), tag=L + f"nm{t}")
            c.op("dve", (lambda t_: lambda v: v.tensor_scalar(
                out=S["bscr"][:, :], in0=S["h"][:, t_, :],
                scalar1=S["st"][:, 1, t_:t_ + 1], scalar2=S["st"][:, 2, t_:t_ + 1],
                op0=ALU.mult, op1=ALU.add))(t),
                reads=(f"h{t}", f"st1_{t}", f"st2_{t}"), writes=("bscr",), tag=L + f"h1_{t}")
            for ch in range(8):
                bank, col = divmod(ch * 128, 512)
                c.op("pe", (lambda c_, b_, co_: lambda te: te.matmul(
                    PS[:, b_, co_:co_ + 128], S["bscr"][:, c_ * 128:(c_ + 1) * 128],
                    S["ident"][:, :], start=True, stop=True))(ch, bank, col),
                    reads=("bscr", "ident"), writes=(f"pb{bank}",), tag=L + f"tr{t}{ch}")
            c.op("act", lambda s: s.copy(
                out=S["bscr2"][:, :], in_=PS[:, 0:2, :].rearrange("p a n -> p (a n)")),
                reads=("pb0", "pb1"), writes=("bscr2",), tag=L + f"ev{t}")
            c.dma("pool", (lambda t_: lambda g: g.dma_start(
                out=Dr["ag_h1_in"].ap().rearrange("(c p) n -> p c n", p=128)[:, :, t_ * 128:(t_ + 1) * 128],
                in_=S["bscr2"][:, :].rearrange("p (c n) -> p c n", n=128)))(t),
                reads=("bscr2",), writes=("ag_h1_in",), stream="g", tag=L + f"agw{t}")
        c.coll(lambda g: g.collective_compute(
            "AllGather", ALU.bypass, replica_groups=groups,
            ins=[Dr["ag_h1_in"].ap().opt()], outs=[Dr["ag_h1_out"].ap().opt()]),
            reads=("ag_h1_in",), writes=("ag_h1_out",), tag=L + "ag")
        TCb = cfg["TC"]
        bidoff = self._rc_mult(0, 0, bmult=G * D * TCb)
        c.dma("sp", (lambda o_: lambda g: g.dma_start(
            out=bass.AP(Dr["h1loc"], 0, [[1, G * D * TCb]]),
            in_=bass.AP(Dr["ag_h1_out"], o_, [[1, G * D * TCb]])))(bidoff),
            reads=("ag_h1_out",), writes=("h1loc",), stream="h", tag=L + "h1st")

    def _mla(self):
        cfg, c, S, Dr, nc = self.cfg, self.c, self.S, self.Dr, self.nc
        NT, G, NKV, TC = cfg["NT"], cfg["G"], cfg["NKV"], cfg["TC"]
        SBK = cfg["SB_KV"]
        NSB = NKV // SBK
        NP = NT // 2
        L = "ml_"
        PS = self.psum("PS", [128, 8, 512], F32)
        pb = lambda i: PS[:, i, :]
        groups = [list(range(cfg["n_cores"]))]
        c.fence(tag=L + "f")
        kscale = self.kk(MLA_SCALE)
        khalf = self.kk(0.5)
        agh = Dr["ag_h1_out"]
        mo = Dr["mla_o"]

        for nm, dn, nch in [("wkvd", "L0_attn_W_kv_down", 8),
                            ("wkup", "L0_attn_W_k_up", 1),
                            ("wvup", "L0_attn_W_v_up", 1),
                            ("wkro", "L0_attn_W_k_rope", 1),
                            ("wq", "L0_attn_W_q", 8),
                            ("wqro", "L0_attn_W_q_rope", 8)]:
            src_ = Dr[dn].ap()
            src_ = (src_.rearrange("(c p) n -> p c n", p=128) if nch > 1
                    else src_.unsqueeze(1))
            c.dma("pool", (lambda n_, s_: lambda g: g.dma_start(
                out=self.S["ml_" + n_][:, :, :], in_=s_))(nm, src_),
                writes=(nm,), stream="w", tag=L + "ld" + nm)
        for nm, dn in [("mcq", "t_mcos_q"), ("msq", "t_msin_q"),
                       ("mck", "t_mcos_k"), ("msk", "t_msin_k")]:
            c.dma("pool", (lambda n_, d_: lambda g: g.dma_start(
                out=self.S["ml_" + n_][:, :, :],
                in_=self.Dr[d_].ap().rearrange("(a p) n -> p a n", p=128)))(nm, dn),
                writes=(nm,), stream="w", tag=L + "ld" + nm)

        if self.sub == "mlw":
            return
        def rotary_tok(dstap, dstkey, srcbank, mmidx, cosn, sinn):
            ca = S["ml_" + cosn][:, mmidx, :]
            sa = S["ml_" + sinn][:, mmidx, :]
            rs = S["ml_rsc"]
            for (o, i0_, tb, tk) in ((0, 0, ca, cosn), (16, 16, sa, sinn),
                                     (32, 16, ca, cosn), (48, 0, sa, sinn)):
                c.op("dve", (lambda o_, i__, t_: lambda v: v.tensor_mul(
                    out=rs[:, o_:o_ + 16], in0=PS[:, srcbank, i__:i__ + 16], in1=t_))(o, i0_, tb),
                    reads=(f"pb{srcbank}", cosn, sinn), writes=("rsc",), tag=L + "ro")
            c.op("dve", (lambda: lambda v: v.tensor_sub(
                out=dstap[:, 0:16], in0=rs[:, 0:16], in1=rs[:, 16:32]))(),
                reads=("rsc",), writes=(dstkey,), tag=L + "rlo")
            c.op("dve", (lambda: lambda v: v.tensor_add(
                out=dstap[:, 16:32], in0=rs[:, 32:48], in1=rs[:, 48:64]))(),
                reads=("rsc",), writes=(dstkey,), tag=L + "rhi")

        for g4 in range(NKV // 4):
            contrib = (g4 * 4) // NT
            colbase = (g4 * 4) % NT * 128
            c.dma("sp", (lambda o_: lambda g: g.dma_start(
                out=S["ml_hrhs"][:, :, :],
                in_=bass.AP(Dr["h1loc"], o_,
                            [[TC, 128], [128 * TC, 8], [1, 512]])))(contrib * 1024 * TC + colbase),
                reads=("h1loc",), writes=("hrhs",), stream="h", tag=L + f"hr{g4}")
            for kc in range(8):
                c.op("pe", (lambda k_: lambda te: te.matmul(
                    pb(5), S["ml_wkvd"][:, k_, :], S["ml_hrhs"][:, k_, :],
                    start=(k_ == 0), stop=(k_ == 7)))(kc),
                    reads=("wkvd", "hrhs"), writes=("pb5",), tag=L + f"ckv{g4}_{kc}")
            c.op("act", (lambda g_: lambda s: s.copy(
                out=S["ml_ckvT"][:, g_ * 512:(g_ + 1) * 512], in_=pb(5)))(g4),
                reads=("pb5",), writes=("ckvT",), tag=L + f"ckve{g4}")

        if self.sub == "mlA":
            return
        for g4 in range(NKV // 4):
            ck = S["ml_ckvT"][:, g4 * 512:(g4 + 1) * 512]
            for oc in range(8):
                c.op("pe", (lambda o_, c_: lambda te: te.matmul(
                    pb(0), S["ml_wkup"][:, 0, o_ * 128:(o_ + 1) * 128], c_,
                    start=True, stop=True))(oc, ck),
                    reads=("wkup", "ckvT"), writes=("pb0",), tag=L + f"kn{g4}_{oc}")
                c.op("act", (lambda: lambda s: s.activation(
                    out=S["ml_sq2"][:, :], in_=pb(0), func=AF.Square))(),
                    reads=("pb0",), writes=("mlsq2",), tag=L + f"knq{g4}_{oc}")
                c.op("pe", (lambda o_: lambda te: te.matmul(
                    PS[0:16, 1, 0:512], S["sel"][:, o_, :], S["ml_sq2"][:, :],
                    start=(o_ == 0), stop=False))(oc),
                    reads=("mlsq2", "sel"), writes=("pb1",), tag=L + f"kns{g4}_{oc}")
            c.op("pe", (lambda c_: lambda te: te.matmul(
                PS[0:32, 0, 0:512], S["ml_wkro"][:, 0, :], c_,
                start=True, stop=True))(ck),
                reads=("wkro", "ckvT"), writes=("pb0",), tag=L + f"kr{g4}")
            c.op("act", (lambda: lambda s: s.activation(
                out=S["ml_sq2"][0:32, :], in_=PS[0:32, 0, 0:512], func=AF.Square))(),
                reads=("pb0",), writes=("mlsq2",), tag=L + f"krq{g4}")
            c.op("pe", (lambda: lambda te: te.matmul(
                PS[0:16, 1, 0:512], S["selrope"][:, :], S["ml_sq2"][0:32, :],
                start=False, stop=True))(),
                reads=("mlsq2", "selrope"), writes=("pb1",), tag=L + f"krs{g4}")
            c.op("dve", (lambda g_: lambda v: v.reduce_max(
                out=S["ml_nrm"][0:16, g_:g_ + 1], in_=PS[0:16, 1, 0:512], axis=AX))(g4),
                reads=("pb1",), writes=("nrm",), tag=L + f"knm{g4}")
        c.op("dve", lambda v: v.reduce_max(
            out=S["ml_nrm"][0:16, 60:61], in_=S["ml_nrm"][0:16, 0:NKV // 4], axis=AX),
            reads=("nrm",), writes=("nrm",), tag=L + "k2a")
        c.op("dve", lambda v: v.tensor_copy(
            out=S["ml_nrmb"][0:16, 0:1], in_=S["ml_nrm"][0:16, 60:61]),
            reads=("nrm",), writes=("nrmb",), tag=L + "k2b")
        c.op("pe", lambda te: te.matmul(
            PS[0:1, 0, 0:16], S["ml_nrmb"][0:16, 0:1], S["ident"][0:16, 0:16],
            start=True, stop=True),
            reads=("nrmb", "ident"), writes=("pb0",), tag=L + "k2t")
        c.op("dve", lambda v: v.reduce_max(
            out=S["ml_nrm"][0:1, 61:62], in_=PS[0:1, 0, 0:16], axis=AX),
            reads=("pb0",), writes=("nrm",), tag=L + "k2c")
        c.op("dve", lambda v: v.tensor_copy(
            out=S["ml_nrmb"][0:1, 1:2], in_=S["ml_nrm"][0:1, 61:62]),
            reads=("nrm",), writes=("nrmb",), tag=L + "k2d")
        c.op("pe", lambda te: te.matmul(
            PS[:, 0, 16:17], S["ones1"][:, :], S["ml_nrmb"][0:1, 1:2],
            start=True, stop=True),
            reads=("ones1", "nrmb"), writes=("pb0",), tag=L + "k2e")
        c.op("dve", lambda v: v.tensor_copy(
            out=S["ml_k2r"][:, 0:1], in_=PS[:, 0, 16:17]),
            reads=("pb0",), writes=("k2r",), tag=L + "k2f")

        if self.sub == "mlB":
            return
        NTq4 = NT // 4
        for ctb in range(4):
            qoff = self._rc_mult(128, ctb * 1024 * TC)
            c.dma("sp", (lambda o_, c2: lambda g: g.dma_start(
                out=bass.AP(Dr["h1q"], c2 * NTq4 * 128, [[128, NTq4], [TC, 1024], [1, 128]]),
                in_=bass.AP(Dr["h1loc"], o_, [[512, NTq4], [TC, 1024], [1, 128]])))(qoff, ctb),
                reads=("h1loc",), writes=("h1q",), stream="h", tag=L + f"qreo{ctb}")
        for mm in range(NT):
            c.dma("sp", (lambda m_: lambda g: g.dma_start(
                out=S["ml_hrhs"][:, :, 0:128],
                in_=bass.AP(Dr["h1q"], m_ * 128, [[TC, 128], [128 * TC, 8], [1, 128]])))(mm),
                reads=("h1q",), writes=("hrhs",), stream="h", tag=L + f"qh{mm}")
            for oc in range(8):
                for kc in range(8):
                    c.op("pe", (lambda o_, k_: lambda te: te.matmul(
                        PS[:, 2, 0:128], S["ml_wq"][:, k_, o_ * 128:(o_ + 1) * 128],
                        S["ml_hrhs"][:, k_, 0:128],
                        start=(k_ == 0), stop=(k_ == 7)))(oc, kc),
                        reads=("wq", "hrhs"), writes=("pb2",), tag=L + f"q{mm}_{oc}_{kc}")
                c.op("act", (lambda o_, m_, k_: lambda s: s.activation(
                    out=S["ml_qT"][:, o_, m_ * 128:(m_ + 1) * 128],
                    in_=PS[:, 2, 0:128], func=AF.Copy, scale=k_))(oc, mm, kscale),
                    reads=("pb2", "konst"), writes=("qT",), tag=L + f"qe{mm}_{oc}")
                c.op("act", (lambda: lambda s: s.activation(
                    out=S["ml_sq2"][:, 0:128], in_=PS[:, 2, 0:128], func=AF.Square))(),
                    reads=("pb2",), writes=("mlsq2",), tag=L + f"qq{mm}_{oc}")
                c.op("pe", (lambda o_: lambda te: te.matmul(
                    PS[0:16, 3, 0:128], S["sel"][:, o_, :], S["ml_sq2"][:, 0:128],
                    start=(o_ == 0), stop=(o_ == 7)))(oc),
                    reads=("mlsq2", "sel"), writes=("pb3",), tag=L + f"qs{mm}_{oc}")
            for kc in range(8):
                c.op("pe", (lambda k_: lambda te: te.matmul(
                    PS[:, 4, 0:32], S["ml_hrhs"][:, k_, 0:128], S["ml_wqro"][:, k_, :],
                    start=(k_ == 0), stop=(k_ == 7)))(kc),
                    reads=("wqro", "hrhs"), writes=("pb4",), tag=L + f"qr{mm}_{kc}")
            rotary_tok(S["ml_rro"][:, 0:32], "rro", 4, mm, "mcq", "msq")
            c.op("act", lambda s: s.activation(
                out=S["ml_sq2"][:, 0:32], in_=S["ml_rro"][:, 0:32], func=AF.Square),
                reads=("rro",), writes=("mlsq2",), tag=L + f"qrq{mm}")
            c.op("dve", lambda v: v.reduce_sum(
                out=S["ml_nrm"][:, 32:33], in_=S["ml_sq2"][:, 0:32], axis=AX),
                reads=("mlsq2",), writes=("nrm",), tag=L + f"qrr{mm}")
            c.op("pe", (lambda m_: lambda te: te.matmul(
                PS[0:32, 5, 0:128], S["ml_rro"][:, 0:32], S["ident"][:, :],
                start=True, stop=True))(mm),
                reads=("rro", "ident"), writes=("pb5",), tag=L + f"qrt{mm}")
            c.op("act", (lambda m_: lambda s: s.copy(
                out=S["ml_qrb"][64:96, m_ * 128:(m_ + 1) * 128],
                in_=PS[0:32, 5, 0:128]))(mm),
                reads=("pb5",), writes=("qrb",), tag=L + f"qre{mm}")
            c.op("act", (lambda m_: lambda s: s.copy(
                out=S["ml_qrb"][0:32, m_ * 128:(m_ + 1) * 128],
                in_=PS[0:32, 5, 0:128]))(mm),
                reads=("pb5",), writes=("qrb",), tag=L + f"qre0{mm}")
            c.op("dve", (lambda: lambda v: v.tensor_copy(
                out=S["ml_nrmb"][0:16, 16:144], in_=PS[0:16, 3, 0:128]))(),
                reads=("pb3",), writes=("nrmb",), tag=L + f"qnb{mm}")
            c.op("pe", lambda te: te.matmul(
                PS[:, 5, 128:144], S["ml_nrmb"][0:16, 16:144], S["ident"][0:16, 0:16],
                start=True, stop=True),
                reads=("nrmb", "ident"), writes=("pb5",), tag=L + f"qnt{mm}")
            c.op("dve", lambda v: v.reduce_max(
                out=S["ml_nrm"][:, 33:34], in_=PS[:, 5, 128:144], axis=AX),
                reads=("pb5",), writes=("nrm",), tag=L + f"qnm{mm}")
            c.op("dve", lambda v: v.tensor_scalar(
                out=S["ml_nrm"][:, 33:34], in0=S["ml_nrm"][:, 33:34],
                scalar1=float(MLA_SCALE * MLA_SCALE), scalar2=None, op0=ALU.mult),
                reads=("nrm",), writes=("nrm",), tag=L + f"qnsc{mm}")
            c.op("dve", lambda v: v.tensor_add(
                out=S["ml_nrm"][:, 34:35], in0=S["ml_nrm"][:, 33:34],
                in1=S["ml_nrm"][:, 32:33]),
                reads=("nrm",), writes=("nrm",), tag=L + f"qna{mm}")
            c.op("dve", lambda v: v.tensor_mul(
                out=S["ml_nrm"][:, 35:36], in0=S["ml_nrm"][:, 34:35],
                in1=S["ml_k2r"][:, 0:1]),
                reads=("nrm", "k2r"), writes=("nrm",), tag=L + f"qnk{mm}")
            c.op("dve", lambda v: v.tensor_scalar_max(
                out=S["ml_nrm"][:, 35:36], in0=S["ml_nrm"][:, 35:36], scalar1=1e-24),
                reads=("nrm",), writes=("nrm",), tag=L + f"qncl{mm}")
            c.op("act", lambda s: s.activation(
                out=S["ml_nrm"][:, 35:36], in_=S["ml_nrm"][:, 35:36], func=AF.Ln),
                reads=("nrm",), writes=("nrm",), tag=L + f"qnl{mm}")
            c.op("act", (lambda k_: lambda s: s.activation(
                out=S["ml_nrm"][:, 36:37], in_=S["ml_nrm"][:, 35:36], func=AF.Exp,
                scale=k_))(khalf),
                reads=("nrm", "konst"), writes=("nrm",), tag=L + f"qnx{mm}")
            c.op("dve", lambda v: v.tensor_copy(
                out=S["ml_nrmb"][:, 2:3], in_=S["ml_nrm"][:, 36:37]),
                reads=("nrm",), writes=("nrmb",), tag=L + f"qnbb{mm}")
            c.op("pe", lambda te: te.matmul(
                PS[0:1, 5, 144:272], S["ml_nrmb"][:, 2:3], S["negident"][:, :],
                start=True, stop=True),
                reads=("nrmb", "negident"), writes=("pb5",), tag=L + f"qnn{mm}")
            c.op("act", lambda s: s.copy(
                out=S["ml_rro"][0:1, 32:160], in_=PS[0:1, 5, 144:272]),
                reads=("pb5",), writes=("rro",), tag=L + f"qnv{mm}")
            c.dma("sp", (lambda m_: lambda g: g.dma_start(
                out=S["ml_qrb"][96:97, m_ * 128:(m_ + 1) * 128],
                in_=S["ml_rro"][0:1, 32:160]))(mm),
                reads=("rro",), writes=("qrb",), stream="s", tag=L + f"qnd{mm}")
            c.dma("sp", (lambda m_: lambda g: g.dma_start(
                out=S["ml_qrb"][32:33, m_ * 128:(m_ + 1) * 128],
                in_=S["ml_rro"][0:1, 32:160]))(mm),
                reads=("rro",), writes=("qrb",), stream="s", tag=L + f"qnd0{mm}")

        if self.sub == "mlC":
            return
        c.fence(tag=L + "fD")
        for j_ in range(SBK):
            for row in (96, 32):
                c.dma("sp", (lambda j2, r2: lambda g: g.dma_start(
                    out=S["ml_krb"][r2:r2 + 1, j2 * 128:(j2 + 1) * 128],
                    in_=S["ones1"][0:1, :]))(j_, row),
                    reads=("ones1",), writes=("krb96",), stream="s",
                    tag=L + f"onesd{j_}_{row}")

        for sb in range(NSB):
            kv0 = sb * SBK
            for gg in range(SBK // 4):
                g4 = (kv0 // 4) + gg
                ck = S["ml_ckvT"][:, g4 * 512:(g4 + 1) * 512]
                for oc in range(8):
                    c.op("pe", (lambda o_, c_: lambda te: te.matmul(
                        pb(0), S["ml_wkup"][:, 0, o_ * 128:(o_ + 1) * 128], c_,
                        start=True, stop=True))(oc, ck),
                        reads=("wkup", "ckvT"), writes=("pb0",), tag=L + f"k{sb}{gg}{oc}")
                    c.op("act", (lambda o_, g_: lambda s: s.copy(
                        out=S["ml_kT"][:, o_, g_ * 512:(g_ + 1) * 512], in_=pb(0)))(oc, gg),
                        reads=("pb0",), writes=("kT",), tag=L + f"ke{sb}{gg}{oc}")
                for j in range(4):
                    kvloc = gg * 4 + j
                    kvt = kv0 + kvloc
                    cks = S["ml_ckvT"][:, kvt * 128:(kvt + 1) * 128]
                    for vg in range(2):
                        c.op("pe", (lambda c_, v_: lambda te: te.matmul(
                            pb(1), c_, S["ml_wvup"][:, 0, v_ * 512:(v_ + 1) * 512],
                            start=True, stop=True))(cks, vg),
                            reads=("ckvT", "wvup"), writes=("pb1",), tag=L + f"v{sb}{kvloc}{vg}")
                        c.op("act", (lambda k_, v_: lambda s: s.copy(
                            out=S["ml_vT"][:, k_, :].rearrange("p (h n) -> p h n", n=65)[
                                :, v_ * 8:(v_ + 1) * 8, 0:64],
                            in_=pb(1).rearrange("p (h n) -> p h n", n=64)))(kvloc, vg),
                            reads=("pb1",), writes=("vT",), tag=L + f"ve{sb}{kvloc}{vg}")
                    c.op("pe", (lambda c_: lambda te: te.matmul(
                        PS[:, 4, 0:32], c_, S["ml_wkro"][:, 0, :],
                        start=True, stop=True))(cks),
                        reads=("ckvT", "wkro"), writes=("pb4",), tag=L + f"krp{sb}{kvloc}")
                    rotary_tok(S["ml_rro"][:, 0:32], "rro", 4, kvt, "mck", "msk")
                    c.op("pe", lambda te: te.matmul(
                        PS[0:32, 5, 0:128], S["ml_rro"][:, 0:32], S["ident"][:, :],
                        start=True, stop=True),
                        reads=("rro", "ident"), writes=("pb5",), tag=L + f"krt{sb}{kvloc}")
                    c.op("act", (lambda k_: lambda s: s.copy(
                        out=S["ml_krb"][64:96, k_ * 128:(k_ + 1) * 128],
                        in_=PS[0:32, 5, 0:128]))(kvloc),
                        reads=("pb5",), writes=("krb",), tag=L + f"kre{sb}{kvloc}")
                    c.op("act", (lambda k_: lambda s: s.copy(
                        out=S["ml_krb"][0:32, k_ * 128:(k_ + 1) * 128],
                        in_=PS[0:32, 5, 0:128]))(kvloc),
                        reads=("pb5",), writes=("krb",), tag=L + f"kre0{sb}{kvloc}")
            c.op("dve", lambda v: v.memset(
                S["ml_vT"][:, :, :].rearrange("p a (h n) -> p a h n", n=65)[:, :, :, 64:65], 1.0),
                writes=("vT",), tag=L + f"vones{sb}")
            if self.sub == "mlDk":
                return
            for qp in range(NP):
                pend = 8 * qp + 7
                if kv0 > pend:
                    continue
                nkvt = min(SBK, pend - kv0 + 1)
                qcols = (2 * qp) * 128
                c.dma("pool", (lambda q_: lambda g: g.dma_start(
                    out=S["ml_vm"][:, :, :],
                    in_=bass.AP(Dr["mla_vmask"], (2 * q_) * 4 * 128 * 128,
                                [[128, 128], [128 * 128, 8], [1, 128]])))(qp),
                    writes=("vm",), stream="w", tag=L + f"vm{sb}{qp}")
                for h in range(NH):
                    avb = 3 + (h % 2)
                    for jj in range(nkvt):
                        kvt = kv0 + jj
                        sbb = 5 + (jj % 2)
                        c.op("pe", (lambda h_, j_, s_, qc_: lambda te: te.matmul(
                            PS[:, s_, 0:256],
                            S["ml_kT"][(h_ % 2) * 64:(h_ % 2) * 64 + 64, h_ // 2,
                                       j_ * 128:(j_ + 1) * 128],
                            S["ml_qT"][(h_ % 2) * 64:(h_ % 2) * 64 + 64, h_ // 2,
                                       qc_:qc_ + 256],
                            start=True, stop=False))(h, jj, sbb, qcols),
                            reads=("kT", "qT"), writes=(f"pb{sbb}",), tag=L + f"S{sb}{qp}{h}{jj}")
                        c.op("pe", (lambda j_, s_, qc_, b_: lambda te: te.matmul(
                            PS[:, s_, 0:256],
                            S["ml_krb"][b_:b_ + 33, j_ * 128:(j_ + 1) * 128],
                            S["ml_qrb"][b_:b_ + 33, qc_:qc_ + 256],
                            start=False, stop=True))(jj, sbb, qcols, (h % 2) * 64),
                            reads=("krb", "qrb", "krb96"), writes=(f"pb{sbb}",),
                            tag=L + f"Sr{sb}{qp}{h}{jj}")
                        c.op("act", (lambda j_, s_: lambda sc: sc.activation(
                            out=S["ml_PT"][:, j_ % 2, :], in_=PS[:, s_, 0:256],
                            func=AF.Exp))(jj, sbb),
                            reads=(f"pb{sbb}",), writes=(f"mPT{jj % 2}",),
                            tag=L + f"E{sb}{qp}{h}{jj}")
                        if self.sub == "mlDs" and qp == 0 and h == 0 and jj == 0:
                            return
                        w0a, w0b = 8 * qp, 8 * qp + 4
                        if kvt >= w0b:
                            c.op("pool", (lambda j_: lambda g: g.memset(
                                S["ml_PT"][:, j_ % 2, 0:128], 0.0))(jj),
                                reads=(f"mPT{jj % 2}",), writes=(f"mPT{jj % 2}",),
                                tag=L + f"Z{sb}{qp}{h}{jj}")
                            c.op("pool", (lambda j_, ix_: lambda g: g.tensor_mul(
                                out=S["ml_PT"][:, j_ % 2, 128:256],
                                in0=S["ml_PT"][:, j_ % 2, 128:256],
                                in1=S["ml_vm"][:, ix_, :]))(jj, 4 + (kvt - w0b)),
                                reads=(f"mPT{jj % 2}", "vm"), writes=(f"mPT{jj % 2}",),
                                tag=L + f"Mb{sb}{qp}{h}{jj}")
                        elif kvt >= w0a:
                            c.op("pool", (lambda j_, ix_: lambda g: g.tensor_mul(
                                out=S["ml_PT"][:, j_ % 2, 0:128],
                                in0=S["ml_PT"][:, j_ % 2, 0:128],
                                in1=S["ml_vm"][:, ix_, :]))(jj, kvt - w0a),
                                reads=(f"mPT{jj % 2}", "vm"), writes=(f"mPT{jj % 2}",),
                                tag=L + f"Ma{sb}{qp}{h}{jj}")
                        c.op("pe", (lambda h_, j_, b_, n_: lambda te: te.matmul(
                            PS[0:65, b_, 0:256],
                            S["ml_vT"][:, j_, h_ * 65:h_ * 65 + 65],
                            S["ml_PT"][:, j_ % 2, :],
                            start=(j_ == 0), stop=(j_ == n_ - 1)))(h, jj, avb, nkvt),
                            reads=(f"mPT{jj % 2}", "vT"), writes=(f"pb{avb}",),
                            tag=L + f"A{sb}{qp}{h}{jj}")
                    c.op("act", (lambda h_, b_: lambda s: s.copy(
                        out=S["ml_ost"][0:65, h_ % 4, :], in_=PS[0:65, b_, 0:256]))(h, avb),
                        reads=(f"pb{avb}",), writes=("ost",), tag=L + f"Ae{sb}{qp}{h}")
                    if self.sub == "mlDh0" and qp == 0 and h == 0:
                        return
                    if h % 4 == 3:
                        hg = h // 4
                        if sb > 0:
                            c.dma("pool", (lambda q_, g_: lambda g2: g2.dma_start(
                                out=S["ml_mlo"][0:65, :, :],
                                in_=bass.AP(mo, (g_ * 4 * 65) * TC + q_ * 256,
                                            [[TC, 65], [65 * TC, 4], [1, 256]])))(qp, hg),
                                reads=("mla_o",), writes=("mlo",), stream="o",
                                tag=L + f"rm{sb}{qp}{hg}")
                            c.op("dve", lambda v: v.tensor_add(
                                out=S["ml_ost"][0:65, :, :], in0=S["ml_ost"][0:65, :, :],
                                in1=S["ml_mlo"][0:65, :, :]),
                                reads=("ost", "mlo"), writes=("ost",),
                                tag=L + f"ra{sb}{qp}{hg}")
                        c.dma("pool", (lambda q_, g_: lambda g2: g2.dma_start(
                            out=bass.AP(mo, (g_ * 4 * 65) * TC + q_ * 256,
                                        [[TC, 65], [65 * TC, 4], [1, 256]]),
                            in_=S["ml_ost"][0:65, :, :]))(qp, hg),
                            reads=("ost",), writes=("mla_o",), stream="o",
                            tag=L + f"sp{sb}{qp}{hg}")

            if self.sub == "mlD0" and sb == 0:
                return
            if self.sub and self.sub.startswith("mlDn") and sb == int(self.sub[4:]) - 1:
                return
        if self.sub == "mlD":
            return
        c.fence(tag=L + "fE")
        c.dma("pool", lambda g: g.dma_start(
            out=S["ml_wo"][:, :, :],
            in_=Dr["L0_attn_W_o"].ap().rearrange("(c p) n -> p c n", p=128)),
            writes=("wo",), stream="w", tag=L + "ldwo")
        for qp in range(NP):
            for par in range(2):
                c.dma("sp", (lambda q_, p_: lambda g: g.dma_start(
                    out=S["ml_obuf"][p_ * 64:(p_ + 1) * 64, :, :],
                    in_=bass.AP(mo, p_ * 65 * TC + q_ * 256,
                                [[TC, 64], [130 * TC, 8], [1, 256]])))(qp, par),
                    reads=("mla_o",), writes=("obuf",), stream="h", tag=L + f"rb{qp}{par}")
            c.dma("sp", (lambda q_: lambda g: g.dma_start(
                out=S["ml_den16"][0:16, 0:256],
                in_=bass.AP(mo, 64 * TC + q_ * 256, [[65 * TC, 16], [1, 256]])))(qp),
                reads=("mla_o",), writes=("den16",), stream="h", tag=L + f"rd{qp}")
            c.op("dve", lambda v: v.reciprocal(
                out=S["ml_den16"][0:16, 256:512], in_=S["ml_den16"][0:16, 0:256]),
                reads=("den16",), writes=("den16",), tag=L + f"rc{qp}")
            c.op("dve", lambda v: v.tensor_copy(
                out=S["ml_denb"][0:16, :], in_=S["ml_den16"][0:16, 256:512]),
                reads=("den16",), writes=("denb",), tag=L + f"rcb{qp}")
            for sl in range(2):
                for hp in range(8):
                    c.op("pe", (lambda hp_, sl_: lambda te: te.matmul(
                        PS[:, 6 + (hp_ % 2), 0:128],
                        S["repsel"][:, hp_ * 128:(hp_ + 1) * 128],
                        S["ml_denb"][0:16, sl_ * 128:(sl_ + 1) * 128],
                        start=True, stop=True))(hp, sl),
                        reads=("denb", "repsel"), writes=(f"pb{6 + hp % 2}",),
                        tag=L + f"rp{qp}{sl}{hp}")
                    c.op("dve", (lambda hp_, sl_: lambda v: v.tensor_mul(
                        out=S["ml_obf"][:, hp_, :],
                        in0=S["ml_obuf"][:, hp_, sl_ * 128:(sl_ + 1) * 128],
                        in1=PS[:, 6 + (hp_ % 2), 0:128]))(hp, sl),
                        reads=("obuf", f"pb{6 + hp % 2}"), writes=("obf",),
                        tag=L + f"rs{qp}{sl}{hp}")
                for og in range(2):
                    for ch in range(8):
                        c.op("pe", (lambda g_, c_: lambda te: te.matmul(
                            PS[:, 2 + g_, :], S["ml_obf"][:, c_, :],
                            S["ml_wo"][:, c_, g_ * 512:(g_ + 1) * 512],
                            start=(c_ == 0), stop=(c_ == 7)))(og, ch),
                            reads=("obf", "wo"), writes=(f"pb{2 + og}",),
                            tag=L + f"wo{qp}{sl}{og}{ch}")
                c.op("act", lambda s: s.copy(
                    out=S["scr1k"][:, :],
                    in_=PS[:, 2:4, :].rearrange("p a n -> p (a n)")),
                    reads=("pb2", "pb3"), writes=("scr1k",), tag=L + f"woe{qp}{sl}")
                c.dma("pool", (lambda q_, s_: lambda g: g.dma_start(
                    out=Dr["ag_mla_in"].ap().rearrange("(a p) n -> p a n", p=128)[
                        :, 2 * q_ + s_, :],
                    in_=S["scr1k"][:, :]))(qp, sl),
                    reads=("scr1k",), writes=("ag_mla_in",), stream="g",
                    tag=L + f"agw{qp}{sl}")
        c.coll(lambda g: g.collective_compute(
            "AllGather", ALU.bypass, replica_groups=groups,
            ins=[Dr["ag_mla_in"].ap().opt()], outs=[Dr["ag_mla_out"].ap().opt()]),
            reads=("ag_mla_in",), writes=("ag_mla_out",), tag=L + "ag")
        bidoff2 = self._rc_mult(0, 0, bmult=G * TC * D)
        c.dma("sp", (lambda o_: lambda g: g.dma_start(
            out=bass.AP(Dr["mlaloc"], 0, [[1, G * TC * D]]),
            in_=bass.AP(Dr["ag_mla_out"], o_, [[1, G * TC * D]])))(bidoff2),
            reads=("ag_mla_out",), writes=("mlaloc",), stream="h", tag=L + "mlst")
        NTq = NT // 4
        for j4 in range(4):
            x2off = self._rc_mult(NTq * 128 * 1024, j4 * TC * 1024)
            c.dma("sp", (lambda o_, j_: lambda g: g.dma_start(
                out=bass.AP(Dr["ag_mla_in"], j_ * 128 * 1024,
                            [[1024, 128], [4 * 128 * 1024, NTq], [1, 1024]]),
                in_=bass.AP(Dr["mlaloc"], o_,
                            [[1024, 128], [128 * 1024, NTq], [1, 1024]])))(x2off, j4),
                reads=("mlaloc", "ag_mla_in"), writes=("x2q",), stream="h",
                tag=L + f"x2re{j4}")
        x2v = Dr["ag_mla_in"].ap().rearrange("(a p) n -> p a n", p=128)
        for j in range(NT):
            c.dma("sp", (lambda j_: lambda g: g.dma_start(
                out=S["scr1k"][:, :], in_=x2v[:, j_]))(j),
                reads=("x2q",), writes=("scr1k",), stream="h", tag=L + f"x2r{j}")
            c.op("dve", (lambda j_: lambda v: v.tensor_add(
                out=S["h"][:, j_, :], in0=S["h"][:, j_, :], in1=S["scr1k"][:, :]))(j),
                reads=(f"h{j}", "scr1k"), writes=(f"h{j}", "scr1k_r"), tag=L + f"x2{j}")

    def _stage_boundary_D(self):
        cfg, c, S, Dr = self.cfg, self.c, self.S, self.Dr
        NT = cfg["NT"]
        L = "bd_"
        pk = Dr["park2"].ap().rearrange("(a p) n -> p a n", p=128)
        for t in range(NT):
            c.dma("pool", (lambda t_: lambda g: g.dma_start(
                out=pk[:, t_], in_=S["h"][:, t_, :]))(t),
                reads=(f"h{t}",), writes=("park2",), stream="o", tag=L + f"pk{t}")
        self._ln_stats_plain(lambda t: f"h{t}", lambda t: S["h"][:, t], 0)
        for t in range(NT):
            c.op("dve", (lambda t_: lambda v: v.scalar_tensor_tensor(
                out=S["st"][:, 2, t_:t_ + 1], in0=S["st"][:, 0, t_:t_ + 1],
                scalar=-1.0, in1=S["st"][:, 1, t_:t_ + 1],
                op0=ALU.mult, op1=ALU.mult))(t),
                reads=(f"st0_{t}", f"st1_{t}"), writes=(f"st2_{t}",), tag=L + f"nm{t}")
            c.op("dve", (lambda t_: lambda v: v.tensor_scalar(
                out=S["h"][:, t_], in0=S["h"][:, t_],
                scalar1=S["st"][:, 1, t_:t_ + 1], scalar2=S["st"][:, 2, t_:t_ + 1],
                op0=ALU.mult, op1=ALU.add))(t),
                reads=(f"h{t}", f"st1_{t}", f"st2_{t}", "park2"), writes=(f"h{t}",),
                tag=L + f"ln{t}")

    def _epilogue(self):
        cfg, c, S, Dr = self.cfg, self.c, self.S, self.Dr
        NT = cfg["NT"]
        L = "ep_"
        pk = Dr["park2"].ap().rearrange("(a p) n -> p a n", p=128)
        ot = Dr["out"].ap().rearrange("(a p) n -> p a n", p=128)
        for t in range(NT):
            c.dma("sp", (lambda t_: lambda g: g.dma_start(
                out=S["scr1k"][:, :], in_=pk[:, t_]))(t),
                reads=("park2",), writes=("scr1k",), stream="x", tag=L + f"ld{t}")
            c.op("dve", (lambda t_: lambda v: v.scalar_tensor_tensor(
                out=S["h"][:, t_, :], in0=S["h"][:, t_, :], scalar=0.5,
                in1=S["scr1k"][:, :], op0=ALU.mult, op1=ALU.add))(t),
                reads=(f"h{t}", "scr1k"), writes=(f"h{t}", "scr1k_r"), tag=L + f"o{t}")
            c.dma("pool", (lambda t_: lambda g: g.dma_start(
                out=ot[:, t_], in_=S["h"][:, t_, :]))(t),
                reads=(f"h{t}",), writes=("outd",), stream="o", tag=L + f"st{t}")



def kernel(**inputs):
    cfg = make_cfg(full=True)
    consts = host_constants(cfg)
    maps = percore_inputs(cfg, inputs, consts)
    b = Build(cfg)
    res = run_bass_kernel_spmd(b.nc, maps, core_ids=list(range(cfg["n_cores"])))
    outs = [r["out"] for r in res.results]
    G, TC = cfg["G"], cfg["TC"]
    full = np.zeros((cfg["B"], cfg["T"], D), np.float32)
    for core in range(cfg["n_cores"]):
        bb, rc = core // G, core % G
        full[bb, rc * TC:(rc + 1) * TC] = outs[core]
    return full
